# version 08 (final)
# speedup vs baseline: 1.3130x; 1.0104x over previous
import numpy as np
import ml_dtypes
from contextlib import ExitStack
import concourse.bass as bass
import concourse.mybir as mybir
from concourse.bass_utils import run_bass_kernel_spmd

F32 = mybir.dt.float32
BF16 = mybir.dt.bfloat16
I32 = mybir.dt.int32
AF = mybir.ActivationFunctionType
ALU = mybir.AluOpType
AX = mybir.AxisListType

T = 2048
D = 1024
NCH = 8
NQ = 4
S127 = float(np.float32(1.0 / 127.0))
S127SQ = float(np.float32(1.0 / 127.0)) ** 2

ENGINES = ("tensor", "vector", "scalar", "gpsimd", "sync")

ALL_STEPS = ["aa0", "f0a", "ab0", "f0b", "aw1", "mo1a", "mb1", "mo1b"]



def _col128(v):
    v = np.asarray(v, dtype=np.float32)
    return np.ascontiguousarray(v.reshape(-1, 128).T)


def _colblocks(w, bc=128):
    K, N = w.shape
    nk = K // 128
    nm = N // bc
    return np.ascontiguousarray(
        w.reshape(nk, 128, nm, bc).transpose(2, 1, 0, 3).reshape(nm * 128, nk * bc))


def _interleave16(w):
    return np.ascontiguousarray(
        np.asarray(w, np.float32).reshape(16, 128, 16).transpose(1, 0, 2).reshape(128, 256))


def _host_tensors(inputs):
    f = np.float32
    Ht = {}

    def add(name, arr, steps):
        Ht[name] = (np.ascontiguousarray(arr), steps)

    add("aa_wq", inputs["L0_attn_a_Wq"].T.astype(f, copy=False), ["aa0"])
    add("aa_wk", inputs["L0_attn_a_Wk"].T.astype(f, copy=False), ["aa0"])
    add("aa_wv", inputs["L0_attn_a_Wv"].T.astype(f, copy=False), ["aa0"])
    add("aa_bq", _col128(inputs["L0_attn_a_bq"]), ["aa0"])
    add("aa_bk", _col128(inputs["L0_attn_a_bk"]), ["aa0"])
    add("aa_bv", np.asarray(inputs["L0_attn_a_bv"], f).reshape(1, D), ["aa0"])

    add("f_wg", _colblocks(np.asarray(inputs["L0_ffn_w_gate"]).T), ["f0a", "f0b"])
    add("f_wu", _colblocks(np.asarray(inputs["L0_ffn_w_up"]).T), ["f0a", "f0b"])
    add("f_wd", np.asarray(inputs["L0_ffn_w_down"]).T.copy(), ["f0a", "f0b"])

    add("ab_wq", inputs["L0_attn_b_q_w"].T.astype(f, copy=False), ["ab0"])
    add("ab_wk", inputs["L0_attn_b_k_w"].T.astype(f, copy=False), ["ab0"])
    add("ab_wv", inputs["L0_attn_b_v_w"].T.astype(f, copy=False), ["ab0"])
    add("ab_wo", inputs["L0_attn_b_o_w"].T.astype(f, copy=False), ["ab0"])
    add("ab_bq", _col128(inputs["L0_attn_b_q_b"]), ["ab0"])
    add("ab_bk", _col128(inputs["L0_attn_b_k_b"]), ["ab0"])
    add("ab_bo", _col128(inputs["L0_attn_b_o_b"]), ["ab0"])
    add("ab_bv", np.asarray(inputs["L0_attn_b_v_b"], f).reshape(1, D), ["ab0"])
    add("ab_qn", _col128(inputs["L0_attn_b_qn_scale"]), ["ab0"])
    add("ab_kn", _col128(inputs["L0_attn_b_kn_scale"]), ["ab0"])

    add("aw_wq", inputs["L1_attn_a_Wq"].T.astype(f, copy=False), ["aw1"])
    add("aw_wk", inputs["L1_attn_a_Wk"].T.astype(f, copy=False), ["aw1"])
    add("aw_wv", inputs["L1_attn_a_Wv"].T.astype(f, copy=False), ["aw1"])
    add("aw_wo", inputs["L1_attn_a_Wo"].T.astype(f, copy=False), ["aw1"])

    add("mo_gw", np.asarray(inputs["L1_ffn_gate_w"], f).T, ["mo1a", "mo1b"])
    ew = np.asarray(inputs["L1_ffn_expert_w"], f).transpose(0, 2, 1)
    add("mo_ew", ew.reshape(16 * 1024, 1024), ["mo1a", "mo1b"])
    add("mo_eb", np.asarray(inputs["L1_ffn_expert_b"], f), ["mo1a", "mo1b"])

    add("mb_gam", _col128(inputs["L1_attn_b_w_norm"]), ["mb1"])
    add("mb_bet", _col128(inputs["L1_attn_b_b_norm"]), ["mb1"])
    add("mb_win", _colblocks(np.asarray(inputs["L1_attn_b_w_in"], f)), ["mb1"])
    wc = np.asarray(inputs["L1_attn_b_w_conv"], f).reshape(16, 128, 4)
    add("mb_wcv", wc.transpose(1, 0, 2).reshape(128, 64), ["mb1"])
    add("mb_bcv", _col128(inputs["L1_attn_b_b_conv"]), ["mb1"])
    add("mb_alog", _interleave16(inputs["L1_attn_b_A_log"]), ["mb1"])
    add("mb_wb", _interleave16(inputs["L1_attn_b_w_b"]), ["mb1"])
    add("mb_wc", _interleave16(inputs["L1_attn_b_w_c"]), ["mb1"])
    add("mb_wdel", _colblocks(np.asarray(inputs["L1_attn_b_w_delta"], f)), ["mb1"])
    add("mb_bdel", _col128(inputs["L1_attn_b_b_delta"]), ["mb1"])
    add("mb_dpar", _col128(inputs["L1_attn_b_D_param"]), ["mb1"])
    add("mb_wout", np.asarray(inputs["L1_attn_b_w_out"], f).copy(), ["mb1"])

    add("c_osum", np.full((128, 128), 1.0 / 1024.0, f), ALL_STEPS)
    add("c_ones_col", np.ones((128, 1), f), ALL_STEPS)
    add("c_ones_row", np.ones((1, 128), f), ALL_STEPS)
    add("c_ident", np.eye(128, dtype=f), ALL_STEPS)
    rperm = np.zeros((128, 128), f)
    for i in range(128):
        rperm[i ^ 1, i] = 1.0
    add("c_rperm", rperm, ["ab0"])
    sel = np.zeros((16, 16 * 128), f)
    for n in range(16):
        sel[n, n * 128:(n + 1) * 128] = 1.0
    add("c_sel16", sel, ["mb1"])
    hd = 64
    invf = 1.0 / (10000.0 ** (np.arange(0, hd, 2, dtype=np.float64) / hd))
    tt = np.arange(T, dtype=np.float64)
    emb = tt[None, :] * np.concatenate([invf, invf])[:, None]
    cos = np.cos(emb)
    sins = np.sin(emb) * np.where(np.arange(hd) % 2 == 0, -1.0, 1.0)[:, None]
    add("c_cos", np.vstack([cos, cos]).astype(f), ["ab0"])
    add("c_sin", np.vstack([sins, sins]).astype(f), ["ab0"])
    i_ = np.arange(128)[:, None]
    j_ = np.arange(128)[None, :]
    wm = np.zeros((128, 384), f)
    for r, rel in enumerate((-1, 0, 1)):
        dd = i_ + 128 * rel - j_
        wm[:, r * 128:(r + 1) * 128] = ((dd >= -8) & (dd <= 7)).astype(f)
    add("c_wmask", wm, ["aw1"])
    return Ht



class Prog:
    def __init__(self):
        self.ops = {e: [] for e in ENGINES}
        self.semc = {}
        self.sem_names = []

    def sem(self, name):
        if name not in self.semc:
            self.semc[name] = 0
            self.sem_names.append(name)
        return name

    def emit(self, eng, fn, waits=(), incs=()):
        for s, a in incs:
            self.semc[s] += a
        self.ops[eng].append((fn, tuple(waits), tuple(incs)))

    def tick(self, eng, fn, waits=(), sem=None):
        s = self.sem(sem or ("s_" + eng))
        self.emit(eng, fn, waits, [(s, 1)])
        return (s, self.semc[s])

    def tok(self, eng):
        s = self.sem("s_" + eng)
        return (s, self.semc[s])


class Ctx:
    def __init__(self, nc, prog, sb, ps, hbm):
        self.nc = nc
        self.p = prog
        self.sb = sb
        self.ps = ps
        self.hbm = hbm

    def load(self, out_ap, in_ap, cast=False, waits=(), key=None):
        p = self.p
        eng = "gpsimd" if cast else "sync"
        if key is None:
            tag = getattr(self, "step_tag", "0")
            sem = p.sem(("q_cc" if cast else "q_cp") + tag)
            p.emit(eng, lambda h, o=out_ap, i=in_ap: h.dma_start(out=o, in_=i),
                   waits, [(sem, 16)])
            return (sem, -1)
        sem = p.sem("q_" + key)
        p.emit(eng, lambda h, o=out_ap, i=in_ap: h.dma_start(out=o, in_=i),
               waits, [(sem, 16)])
        return (sem, p.semc[sem])

    def store(self, out_ap, in_ap, waits=()):
        p = self.p
        sem = p.sem("q_st")
        p.emit("sync", lambda sy, o=out_ap, i=in_ap: sy.dma_start(out=o, in_=i),
               waits, [(sem, 16)])
        return (sem, p.semc[sem])

    def mm_acc(self, out_ap, pairs, waits=()):
        p = self.p
        n = len(pairs)
        tok = None
        for i, (lh, rh) in enumerate(pairs):
            w = waits if i == 0 else ()
            fn = (lambda t, o=out_ap, l=lh, r=rh, st=(i == 0), sp=(i == n - 1):
                  t.matmul(out=o, lhsT=l, rhs=r, start=st, stop=sp))
            if i == n - 1:
                tok = p.tick("tensor", fn, w)
            else:
                p.emit("tensor", fn, w)
        return tok


def XRc(C, c):
    return C.sb["XR"][:, c * T:(c + 1) * T]


def XHc(C, c):
    return C.sb["XH"][:, c * T:(c + 1) * T]



def emit_load_x(C):
    toks = [C.load(C.sb["XR"][:, c * T:(c + 1) * T],
                   C.hbm["x_t"][c * 128:(c + 1) * 128, :], key="x")
            for c in range(NCH)]
    return [toks[-1]]


def emit_store_x(C, waits):
    toks = []
    for c in range(NCH):
        toks.append(C.store(C.hbm["y_t"][c * 128:(c + 1) * 128, :],
                            C.sb["XR"][:, c * T:(c + 1) * T],
                            waits=waits if c == 0 else ()))
    return toks


def emit_ln(C, dst="XH", eps=1e-5, src_waits=(), const_waits=()):
    p = C.p
    sb = C.sb
    xb, sq = sb["lnxb"], sb["lnsq"]
    mu, msq, var = sb["lnmu"], sb["lnmsq"], sb["lnvar"]
    abc, bbc, tmp = sb["lnA"], sb["lnB"], sb["lntmp"]
    osum = sb["c_osum"]
    out_toks = []
    for q in range(NQ):
        cs = slice(q * 512, (q + 1) * 512)
        vtok = None
        for c in range(NCH):
            vtok = p.tick("vector", lambda v, c=c, cs=cs: v.tensor_copy(
                out=xb[:, c * 512:(c + 1) * 512], in_=XRc(C, c)[:, cs]),
                waits=src_waits if (q == 0 and c == 0) else ())
        stok = None
        for c in range(NCH):
            stok = p.tick("scalar", lambda s, c=c: s.activation(
                out=sq[:, c * 512:(c + 1) * 512],
                in_=xb[:, c * 512:(c + 1) * 512], func=AF.Square),
                waits=[vtok] if c == 0 else ())
        bmu = C.ps[q * 2]
        bm2 = C.ps[q * 2 + 1]
        mmw = [vtok] + (list(const_waits) if q == 0 else [])
        tok_mu = C.mm_acc(bmu[:, :], [(osum[:, :], xb[:, c * 512:(c + 1) * 512])
                                      for c in range(NCH)], waits=mmw)
        tok_m2 = C.mm_acc(bm2[:, :], [(osum[:, :], sq[:, c * 512:(c + 1) * 512])
                                      for c in range(NCH)], waits=[stok])
        t1 = p.tick("vector", lambda v, bmu=bmu: v.tensor_copy(
            out=mu[:, :], in_=bmu[:, :]), waits=[tok_mu])
        t2 = p.tick("scalar", lambda s: s.activation(out=msq[:, :], in_=mu[:, :],
                                                     func=AF.Square), waits=[t1])
        t3 = p.tick("vector", lambda v, bm2=bm2: v.tensor_tensor(
            out=var[:, :], in0=bm2[:, :], in1=msq[:, :], op=ALU.subtract),
            waits=[tok_m2, t2])
        epsi = {1e-5: 0, 1e-8: 1}[eps]
        t4 = p.tick("scalar", lambda s, epsi=epsi: s.activation(
            out=var[:, :], in_=var[:, :], func=AF.Ln,
            bias=sb["cpool"][:, epsi:epsi + 1]),
            waits=[t3, C.cpool_tok] if q == 0 else [t3])
        t5 = p.tick("scalar", lambda s: s.activation(out=abc[:, :], in_=var[:, :],
                                                     func=AF.Exp, scale=-0.5),
                    waits=[t4])
        t6 = p.tick("vector", lambda v: v.scalar_tensor_tensor(
            out=bbc[:, :], in0=mu[:, :], scalar=-1.0, in1=abc[:, :],
            op0=ALU.mult, op1=ALU.mult), waits=[t5])
        tlast = None
        for c in range(NCH):
            p.emit("vector", lambda v, c=c: v.tensor_tensor(
                out=tmp[:, :], in0=xb[:, c * 512:(c + 1) * 512], in1=abc[:, :],
                op=ALU.mult), waits=[t6] if c == 0 else ())
            tlast = p.tick("vector", lambda v, c=c, q=q: v.tensor_tensor(
                out=C.sb[dst][:, c * T + q * 512: c * T + q * 512 + 512],
                in0=tmp[:, :], in1=bbc[:, :], op=ALU.add))
        out_toks.append(tlast)
    return out_toks


class Banks:

    def __init__(self, ids):
        self.ids = list(ids)
        self.free = {i: None for i in self.ids}
        self.rr = 0

    def acquire(self):
        i = self.ids[self.rr % len(self.ids)]
        self.rr += 1
        t = self.free[i]
        return i, ([t] if t else [])

    def release(self, i, tok):
        self.free[i] = tok


def emit_attn_a(C, xh_toks, wtoks):
    p = C.p
    sb = C.sb
    wbuf = sb["aa_wb"]
    qt, kt, vt = sb["aa_qt"], sb["aa_kt"], sb["aa_v"]
    pt = sb["aa_pt"]
    ocol, orow = sb["c_ones_col"], sb["c_ones_row"]
    res_toks = []
    pt_guard = {}
    guard = {m: None for m in range(8)}
    wslot_tok = {}

    def wload(name, c, slot, extra=()):
        tokp = wslot_tok.get(slot)
        return C.load(wbuf[:, slot * 1024:(slot + 1) * 1024],
                      C.hbm[name][c * 128:(c + 1) * 128, :], cast=True,
                      waits=([tokp] if tokp else []) + list(extra),
                      key=f"wb{slot}")

    for Q in range(NQ):
        qs = slice(Q * 512, (Q + 1) * 512)
        ev_last = None
        for name, dst, bias in (("aa_wq", qt, sb["aa_bq"]),
                                ("aa_wk", kt, sb["aa_bk"])):
            mm_last = {}
            for ci in range(NCH):
                slot = ci % 2
                lw = wload(name, ci, slot,
                           extra=wtoks if (Q == 0 and ci == 0
                                           and name == "aa_wq") else ())
                for m in range(8):
                    w = [lw]
                    if ci == 0:
                        w.append(xh_toks[Q])
                        if guard[m] is not None:
                            w.append(guard[m])
                        elif Q == 0:
                            w.append(xh_toks[m // 2])
                    fn = (lambda t, o=C.ps[m][:, :],
                          l=wbuf[:, slot * 1024 + m * 128: slot * 1024 + (m + 1) * 128],
                          r=XHc(C, ci)[:, qs],
                          st=(ci == 0), sp=(ci == 7):
                          t.matmul(out=o, lhsT=l, rhs=r, start=st, stop=sp))
                    if m == 7 or ci == 7:
                        tokm = p.tick("tensor", fn, w)
                        if m == 7:
                            wslot_tok[slot] = tokm
                        if ci == 7:
                            mm_last[m] = tokm
                    else:
                        p.emit("tensor", fn, w)
            for m in range(8):
                ev = p.tick("scalar", lambda s, m=m, dst=dst, bias=bias:
                            s.activation(out=dst[:, m * 512:(m + 1) * 512],
                                         in_=C.ps[m][:, :], func=AF.Identity,
                                         bias=bias[:, m:m + 1]), waits=[mm_last[m]])
                guard[m] = ev
                ev_last = ev
        for ci in range(NCH):
            slot = ci % 2
            lw = wload("aa_wv", ci, slot)
            for bi in range(8):
                tt, vh = bi // 2, bi % 2
                w = [lw]
                if ci == 0 and guard[bi] is not None:
                    w.append(guard[bi])
                fn = (lambda t, o=C.ps[bi][:, :],
                      l=XHc(C, ci)[:, Q * 512 + tt * 128: Q * 512 + (tt + 1) * 128],
                      r=wbuf[:, slot * 1024 + vh * 512: slot * 1024 + vh * 512 + 512],
                      st=(ci == 0): t.matmul(out=o, lhsT=l, rhs=r,
                                             start=st, stop=False))
                if bi == 7:
                    wslot_tok[slot] = p.tick("tensor", fn, w)
                else:
                    p.emit("tensor", fn, w)
        mm_toks = {}
        for bi in range(8):
            vh = bi % 2
            mm_toks[bi] = p.tick("tensor", lambda t, bi=bi, vh=vh: t.matmul(
                out=C.ps[bi][:, :], lhsT=orow[:, :],
                rhs=sb["aa_bv"][:, vh * 512:vh * 512 + 512],
                start=False, stop=True))
        for bi in range(8):
            tt, vh = bi // 2, bi % 2
            evt = p.tick("vector", lambda v, bi=bi, tt=tt, vh=vh: v.tensor_copy(
                out=vt[:, tt * 1024 + vh * 512: tt * 1024 + vh * 512 + 512],
                in_=C.ps[bi][:, :]), waits=[mm_toks[bi]])
            guard[bi] = evt
        apool = Banks([4, 5, 6, 7])
        for i in apool.ids:
            apool.free[i] = guard[i]
        for ch in range(4):
            slot = ch % 2
            bi, bw = apool.acquire()
            bank_s = C.ps[bi]
            w = bw + [ev_last] if ch == 0 else bw
            if ch == 0:
                w = list(w) + [p.tok("vector")]
            ts = C.mm_acc(bank_s[:, 0:128],
                          [(kt[:, c2 * 512 + ch * 128: c2 * 512 + (ch + 1) * 128],
                            qt[:, c2 * 512 + ch * 128: c2 * 512 + (ch + 1) * 128])
                           for c2 in range(NCH)], waits=w)
            pw = [ts] + ([pt_guard[slot]] if slot in pt_guard else [])
            te_ = p.tick("scalar", lambda s, bank_s=bank_s, slot=slot: s.activation(
                out=pt[:, slot * 128:(slot + 1) * 128], in_=bank_s[:, 0:128],
                func=AF.Exp, bias=sb["cpool"][:, 2:3]), waits=pw)
            apool.release(bi, te_)
            bj, bjw = apool.acquire()
            bank_n = C.ps[bj]
            tn = C.mm_acc(bank_n[0:1, 0:128],
                          [(ocol[:, :], pt[:, slot * 128:(slot + 1) * 128])],
                          waits=bjw + [te_])
            nrf = sb["aa_nrf"]
            tr = p.tick("vector", lambda v, bank_n=bank_n, slot=slot: v.reciprocal(
                out=nrf[0:1, slot * 128:(slot + 1) * 128],
                in_=bank_n[0:1, 0:128]), waits=[tn])
            apool.release(bj, tr)
            bk_, bkw = apool.acquire()
            bank_b = C.ps[bk_]
            tb = C.mm_acc(bank_b[:, 0:128],
                          [(sb["c_ones_row_f"][:, :],
                            nrf[0:1, slot * 128:(slot + 1) * 128])],
                          waits=bkw + [tr])
            tnm = p.tick("vector", lambda v, bank_b=bank_b, slot=slot: v.tensor_tensor(
                out=pt[:, slot * 128:(slot + 1) * 128],
                in0=pt[:, slot * 128:(slot + 1) * 128],
                in1=bank_b[:, 0:128], op=ALU.mult), waits=[tb])
            apool.release(bk_, tnm)
            g0 = Q * 512 + ch * 128
            last_mm = None
            for c2 in range(NCH):
                bi2, bw2 = apool.acquire()
                bank_a = C.ps[bi2]
                ta = C.mm_acc(bank_a[:, 0:128],
                              [(vt[:, ch * 1024 + c2 * 128: ch * 1024 + (c2 + 1) * 128],
                                pt[:, slot * 128:(slot + 1) * 128])],
                              waits=bw2 + ([tnm] if c2 == 0 else []))
                tx = p.tick("vector", lambda v, bank_a=bank_a, c2=c2, g0=g0:
                            v.tensor_tensor(out=XRc(C, c2)[:, g0:g0 + 128],
                                            in0=XRc(C, c2)[:, g0:g0 + 128],
                                            in1=bank_a[:, 0:128], op=ALU.add),
                            waits=[ta])
                apool.release(bi2, tx)
                res_toks.append(tx)
                last_mm = ta
            pt_guard[slot] = last_mm
        for i in apool.ids:
            guard[i] = apool.free[i]
    return res_toks


def emit_ln_coef(C, src_waits=(), const_waits=(), rms_eps=None):
    p = C.p
    sb = C.sb
    xb, sq = sb["lnxb"], sb["lnsq"]
    mu, msq, var = sb["lnmu"], sb["lnmsq"], sb["lnvar"]
    ab = sb["ab_ab"]
    osum = sb["c_osum"]
    out_toks = []
    r2_toks = []
    for q in range(NQ):
        cs = slice(q * 512, (q + 1) * 512)
        vtok = None
        for c in range(NCH):
            vtok = p.tick("vector", lambda v, c=c, cs=cs: v.tensor_copy(
                out=xb[:, c * 512:(c + 1) * 512], in_=XRc(C, c)[:, cs]),
                waits=src_waits if (q == 0 and c == 0) else ())
        stok = None
        for c in range(NCH):
            stok = p.tick("scalar", lambda s, c=c: s.activation(
                out=sq[:, c * 512:(c + 1) * 512],
                in_=xb[:, c * 512:(c + 1) * 512], func=AF.Square),
                waits=[vtok] if c == 0 else ())
        bmu = C.ps[q * 2]
        bm2 = C.ps[q * 2 + 1]
        mmw = [vtok] + (list(const_waits) if q == 0 else [])
        tok_mu = C.mm_acc(bmu[:, :], [(osum[:, :], xb[:, c * 512:(c + 1) * 512])
                                      for c in range(NCH)], waits=mmw)
        tok_m2 = C.mm_acc(bm2[:, :], [(osum[:, :], sq[:, c * 512:(c + 1) * 512])
                                      for c in range(NCH)], waits=[stok])
        t1 = p.tick("vector", lambda v, bmu=bmu: v.tensor_copy(
            out=mu[:, :], in_=bmu[:, :]), waits=[tok_mu])
        t2 = p.tick("scalar", lambda s: s.activation(out=msq[:, :], in_=mu[:, :],
                                                     func=AF.Square), waits=[t1])
        t3 = p.tick("vector", lambda v, bm2=bm2: v.tensor_tensor(
            out=var[:, :], in0=bm2[:, :], in1=msq[:, :], op=ALU.subtract),
            waits=[tok_m2, t2])
        t4 = p.tick("scalar", lambda s: s.activation(
            out=msq[:, :], in_=var[:, :], func=AF.Ln,
            bias=sb["cpool"][:, 0:1]),
            waits=[t3, C.cpool_tok] if q == 0 else [t3])
        t5 = p.tick("scalar", lambda s, cs=cs: s.activation(
            out=ab[:, 0 * T:1 * T][:, cs], in_=msq[:, :],
            func=AF.Exp, scale=-0.5), waits=[t4])
        t6 = p.tick("vector", lambda v, cs=cs: v.scalar_tensor_tensor(
            out=ab[:, 1 * T:2 * T][:, cs], in0=mu[:, :], scalar=-1.0,
            in1=ab[:, 0 * T:1 * T][:, cs], op0=ALU.mult, op1=ALU.mult),
            waits=[t5])
        tlast = t6
        if rms_eps is not None:
            t7 = p.tick("vector", lambda v, cs=cs: v.tensor_tensor(
                out=msq[:, :], in0=var[:, :], in1=ab[:, 0 * T:1 * T][:, cs],
                op=ALU.mult), waits=[t6])
            t8 = p.tick("vector", lambda v, cs=cs: v.tensor_tensor(
                out=msq[:, :], in0=msq[:, :], in1=ab[:, 0 * T:1 * T][:, cs],
                op=ALU.mult), waits=[t7])
            ei2 = {1e-8: 1, 1e-5: 0}[rms_eps]
            t9 = p.tick("scalar", lambda s, ei2=ei2: s.activation(
                out=msq[:, :], in_=msq[:, :], func=AF.Ln,
                bias=sb["cpool"][:, ei2:ei2 + 1]), waits=[t8])
            t10 = p.tick("scalar", lambda s, cs=cs: s.activation(
                out=sb["ab_r2"][:, cs], in_=msq[:, :], func=AF.Exp, scale=-0.5),
                waits=[t9])
            r2_toks.append(t10)
        out_toks.append(tlast)
    return (out_toks, r2_toks) if rms_eps is not None else out_toks


def _emit_xhat(C, xv, coef, q, waits):
    p = C.p
    sb = C.sb
    ab = sb["ab_ab"]
    cs = slice(q * 512, (q + 1) * 512)
    tok = None
    for c in range(NCH):
        p.emit("vector", lambda v, c=c, cs=cs, coef=coef: v.tensor_tensor(
            out=sb["ab_xt"][:, :], in0=XRc(C, c)[:, cs],
            in1=ab[:, coef * T:(coef + 1) * T][:, cs], op=ALU.mult),
            waits=waits if c == 0 else ())
        tok = p.tick("vector", lambda v, c=c, cs=cs, coef=coef: v.tensor_tensor(
            out=xv[:, c * 512:(c + 1) * 512], in0=sb["ab_xt"][:, :],
            in1=ab[:, (coef + 1) * T:(coef + 2) * T][:, cs], op=ALU.add))
    return tok


def emit_attn_b(C, ln_toks, r2_toks, wtoks):
    p = C.p
    sb = C.sb
    wbuf = sb["ab_w"]
    kt_sb = sb["ab_kt"]
    qt_sb = sb["ab_qt"]
    va = sb["ab_va"]
    ptb = sb["ab_pt"]
    C.pcnt = 0
    qr = sb["ab_qr"]
    att_e = sb["ab_ae"]
    nrf = sb["ab_nrf"]
    rope_cos = sb["ab_ro"][:, 0:T]
    rope_sin = sb["ab_ro"][:, T:2 * T]
    xv = sb["ab_xv"]
    ab = sb["ab_ab"]
    r2t = sb["ab_r2"]
    res_toks = []
    wslot_tok = {}
    guard = {m: None for m in range(8)}

    def wload(name, c, slot, extra=()):
        tokp = wslot_tok.get(slot)
        return C.load(wbuf[:, slot * 1024:(slot + 1) * 1024],
                      C.hbm[name][c * 128:(c + 1) * 128, :], cast=True,
                      waits=([tokp] if tokp else []) + list(extra),
                      key=f"wb{slot}")

    tva0 = p.tick("vector", lambda v: v.memset(va[:, :], 1.0),
                  waits=[ln_toks[-1]])
    ev = {}
    for q in range(NQ):
        xtok = _emit_xhat(C, xv, 0, q,
                          waits=[tva0] + list(wtoks) if q == 0 else ())
        mm_toks = {}
        for ci in range(NCH):
            lw = wload("ab_wv", ci, ci % 2, extra=wtoks if (q == 0 and ci == 0) else ())
            for bi in range(8):
                tt, vh = bi // 2, bi % 2
                w = [lw]
                if ci == 0:
                    w.append(xtok)
                    if guard[bi] is not None:
                        w.append(guard[bi])
                fn = (lambda t, o=C.ps[bi][:, :],
                      l=xv[:, ci * 512 + tt * 128: ci * 512 + (tt + 1) * 128],
                      r=wbuf[:, (ci % 2) * 1024 + vh * 512:(ci % 2) * 1024 + vh * 512 + 512],
                      st=(ci == 0): t.matmul(out=o, lhsT=l, rhs=r, start=st, stop=False))
                if bi == 7:
                    wslot_tok[ci % 2] = p.tick("tensor", fn, w)
                else:
                    p.emit("tensor", fn, w)
        for bi in range(8):
            tt, vh = bi // 2, bi % 2
            mm_toks[bi] = p.tick("tensor", lambda t, bi=bi, vh=vh: t.matmul(
                out=C.ps[bi][:, :], lhsT=sb["c_ones_row"][:, :],
                rhs=sb["ab_bv"][:, vh * 512:vh * 512 + 512],
                start=False, stop=True))
        for bi in range(8):
            tt, vh = bi // 2, bi % 2
            tt_g = q * 4 + tt
            dst = va[:, tt_g * 1040 + vh * 520: tt_g * 1040 + vh * 520 + 520]
            dst = dst.rearrange("p (h c) -> p h c", c=65)[:, :, 0:64]
            src = C.ps[bi][:, :].rearrange("p (h c) -> p h c", c=64)
            evt = p.tick("vector", lambda v, dst=dst, src=src: v.tensor_copy(
                out=dst, in_=src), waits=[mm_toks[bi]])
            guard[bi] = evt
            ev[(q, bi)] = evt

    conv_tok = None
    for q in range(NQ):
        cs = slice(q * 512, (q + 1) * 512)
        p.emit("vector", lambda v, cs=cs: v.tensor_tensor(
            out=ab[:, 0 * T:1 * T][:, cs], in0=ab[:, 0 * T:1 * T][:, cs],
            in1=r2t[:, cs], op=ALU.mult),
            waits=[r2_toks[q]] if q == 0 else ())
        conv_tok = p.tick("vector", lambda v, cs=cs: v.tensor_tensor(
            out=ab[:, 1 * T:2 * T][:, cs], in0=ab[:, 1 * T:2 * T][:, cs],
            in1=r2t[:, cs], op=ALU.mult))

    def wload_scaled(name, c, slot, scale_sb):
        lt = wload(name, c, slot)
        return p.tick("scalar", lambda s, slot=slot, c=c, scale_sb=scale_sb:
                      s.activation(out=wbuf[:, slot * 1024:(slot + 1) * 1024],
                                   in_=wbuf[:, slot * 1024:(slot + 1) * 1024],
                                   func=AF.Identity, scale=scale_sb[:, c:c + 1]),
                      waits=[lt])

    qr_guard = {0: None, 1: None}

    def proj_rope(dst_cols, q, bias_sb, wname, scale_sb, xw):
        cs = slice(q * 512, (q + 1) * 512)
        xtok = _emit_xhat(C, xv, 0, q, waits=xw)
        mm_last = {}
        for ci in range(NCH):
            slot = ci % 2
            lw = wload_scaled(wname, ci, slot, scale_sb)
            for m in range(8):
                w = [lw]
                if ci == 0:
                    w.append(xtok)
                    if guard[m] is not None:
                        w.append(guard[m])
                fn = (lambda t, o=C.ps[m][:, :],
                      l=wbuf[:, slot * 1024 + m * 128: slot * 1024 + (m + 1) * 128],
                      r=xv[:, ci * 512:(ci + 1) * 512],
                      st=(ci == 0), sp=(ci == 7):
                      t.matmul(out=o, lhsT=l, rhs=r, start=st, stop=sp))
                if m == 7 or ci == 7:
                    tokm = p.tick("tensor", fn, w)
                    if m == 7:
                        wslot_tok[slot] = tokm
                    if ci == 7:
                        mm_last[m] = tokm
                else:
                    p.emit("tensor", fn, w)
        last = None
        for m in range(8):
            slot = m % 2
            g1 = [mm_last[m]] + ([qr_guard[slot]] if qr_guard[slot] else [])
            t1 = p.tick("scalar", lambda s, m=m, slot=slot, bias_sb=bias_sb:
                        s.activation(out=qr[:, slot * 512:(slot + 1) * 512],
                                     in_=C.ps[m][:, :], func=AF.Identity,
                                     bias=bias_sb[:, m:m + 1]), waits=g1)
            t2 = p.tick("tensor", lambda t, m=m, slot=slot: t.matmul(
                out=C.ps[m][:, :], lhsT=sb["c_rperm"][:, :],
                rhs=qr[:, slot * 512:(slot + 1) * 512], start=True, stop=True),
                waits=[t1])
            t3 = p.tick("vector", lambda v, m=m, cs=cs: v.tensor_tensor(
                out=sb["ab_xt"][:, :], in0=C.ps[m][:, :], in1=rope_sin[:, cs],
                op=ALU.mult), waits=[t2])
            p.emit("vector", lambda v, slot=slot, cs=cs: v.tensor_tensor(
                out=qr[:, slot * 512:(slot + 1) * 512],
                in0=qr[:, slot * 512:(slot + 1) * 512], in1=rope_cos[:, cs],
                op=ALU.mult))
            dstap = dst_cols(m)
            t5 = p.tick("vector", lambda v, dstap=dstap, slot=slot: v.tensor_tensor(
                out=dstap, in0=sb["ab_xt"][:, :],
                in1=qr[:, slot * 512:(slot + 1) * 512], op=ALU.add))
            qr_guard[slot] = t5
            guard[m] = t3
            last = t5
        return last

    klast = None
    for q in range(NQ):
        def kdst(m, q=q):
            return kt_sb[:, m * T + q * 512: m * T + q * 512 + 512]
        xw = [conv_tok] if q == 0 else []
        klast = proj_rope(kdst, q, sb["ab_bk_sb"], "ab_wk", sb["ab_kn_sb"], xw)

    pt_guard = {0: None, 1: None}
    ae_guard = {0: None, 1: None}
    for qq in range(4):
        def qdst(m):
            return qt_sb[:, m * 512:(m + 1) * 512]
        xw = [res_toks[-1]] if res_toks else []
        qlast = proj_rope(qdst, qq, sb["ab_bq_sb"], "ab_wq", sb["ab_qn_sb"], xw)
        apool = Banks([0, 1, 2, 3, 4, 5])
        for i in apool.ids:
            apool.free[i] = guard[i]
        att_guard = {0: guard[6], 1: guard[7]}
        for h in range(16):
            r0 = 64 * (h % 2)
            mcol = h // 2
            abk = 6 + (h % 2)
            slot = h % 2
            LAG = 3
            tp_k = {}
            pv_tok = None
            for kk in range(16 + LAG):
                if kk < 16:
                    kti = kk
                    psl = C.pcnt % 4
                    C.pcnt += 1
                    bi, bw = apool.acquire()
                    w = list(bw)
                    if h == 0 and kti == 0:
                        w.append(qlast)
                    ts_ = p.tick("tensor", lambda t, bi=bi, r0=r0, mcol=mcol,
                                 kti=kti: t.matmul(
                        out=C.ps[bi][:, :],
                        lhsT=kt_sb[r0:r0 + 64,
                                   mcol * T + kti * 128: mcol * T + (kti + 1) * 128],
                        rhs=qt_sb[r0:r0 + 64, mcol * 512: mcol * 512 + 512],
                        start=True, stop=True), waits=w)
                    pw = [ts_] + ([pt_guard[psl]] if pt_guard.get(psl) else [])
                    tp = p.tick("scalar", lambda s, bi=bi, psl=psl: s.activation(
                        out=ptb[:, psl * 512:(psl + 1) * 512], in_=C.ps[bi][:, :],
                        func=AF.Sigmoid, scale=0.125), waits=pw)
                    apool.release(bi, tp)
                    tp_k[kti] = (tp, psl)
                if kk >= LAG:
                    kti = kk - LAG
                    tp, psl = tp_k.pop(kti)
                    pvw = [tp]
                    if kti == 0 and att_guard[slot] is not None:
                        pvw.append(att_guard[slot])
                    pv_tok = p.tick("tensor", lambda t, abk=abk, h=h, kti=kti,
                                    psl=psl: t.matmul(
                        out=C.ps[abk][0:65, :],
                        lhsT=va[:, kti * 1040 + h * 65: kti * 1040 + (h + 1) * 65],
                        rhs=ptb[:, psl * 512:(psl + 1) * 512],
                        start=(kti == 0), stop=(kti == 15)), waits=pvw)
                    pt_guard[psl] = pv_tok
            tn1 = p.tick("vector", lambda v, abk=abk, slot=slot: v.tensor_scalar(
                out=nrf[0:1, slot * 512:(slot + 1) * 512],
                in0=C.ps[abk][64:65, :], scalar1=1e-8, scalar2=None,
                op0=ALU.add), waits=[pv_tok])
            tn2 = p.tick("vector", lambda v, slot=slot: v.reciprocal(
                out=nrf[0:1, slot * 512:(slot + 1) * 512],
                in_=nrf[0:1, slot * 512:(slot + 1) * 512]))
            tev = p.tick("vector", lambda v, abk=abk, slot=slot: v.tensor_copy(
                out=att_e[:, slot * 512:(slot + 1) * 512],
                in_=C.ps[abk][0:64, :]),
                waits=([ae_guard[slot]] if ae_guard[slot] else []))
            bj, bjw = apool.acquire()
            tb = p.tick("tensor", lambda t, bj=bj, slot=slot: t.matmul(
                out=C.ps[bj][0:64, :], lhsT=sb["c_ones_row_f"][0:1, 0:64],
                rhs=nrf[0:1, slot * 512:(slot + 1) * 512],
                start=True, stop=True), waits=bjw + [tn2])
            ta = p.tick("vector", lambda v, bj=bj, slot=slot, r0=r0,
                        mcol=mcol: v.tensor_tensor(
                out=qt_sb[r0:r0 + 64, mcol * 512: mcol * 512 + 512],
                in0=att_e[:, slot * 512:(slot + 1) * 512],
                in1=C.ps[bj][0:64, :], op=ALU.mult), waits=[tb, tev])
            apool.release(bj, ta)
            ae_guard[slot] = ta
            att_guard[slot] = ta
        for i in apool.ids:
            guard[i] = apool.free[i]
        guard[6], guard[7] = att_guard[0], att_guard[1]
        omm = {}
        for ci in range(NCH):
            slot = ci % 2
            lw = wload("ab_wo", ci, slot)
            for m in range(8):
                w = [lw]
                if ci == 0:
                    w.append(att_guard[0])
                    w.append(att_guard[1])
                    if guard[m] is not None:
                        w.append(guard[m])
                fn = (lambda t, o=C.ps[m][:, :],
                      l=wbuf[:, slot * 1024 + m * 128: slot * 1024 + (m + 1) * 128],
                      r=qt_sb[:, ci * 512:(ci + 1) * 512],
                      st=(ci == 0), sp=(ci == 7):
                      t.matmul(out=o, lhsT=l, rhs=r, start=st, stop=sp))
                if ci == 7 or m == 7:
                    tokm = p.tick("tensor", fn, w)
                    if m == 7:
                        wslot_tok[slot] = tokm
                    if ci == 7:
                        omm[m] = tokm
                else:
                    p.emit("tensor", fn, w)
        for m in range(8):
            gcol = qq * 512
            tx = p.tick("vector", lambda v, m=m, gcol=gcol: v.scalar_tensor_tensor(
                out=XRc(C, m)[:, gcol:gcol + 512], in0=C.ps[m][:, :],
                scalar=sb["ab_bo_sb"][:, m:m + 1],
                in1=XRc(C, m)[:, gcol:gcol + 512],
                op0=ALU.add, op1=ALU.add), waits=[omm[m]])
            guard[m] = tx
            res_toks.append(tx)
    return res_toks


def emit_attn_w(C, xh_toks, wtoks):
    p = C.p
    sb = C.sb
    wbuf = sb["aw_w"]
    kt_sb = sb["aw_kt"]
    qt_sb = sb["aw_qt"]
    va = sb["aw_va"]
    ptb = sb["aw_pt"]
    C.pcnt = 0
    att_e = sb["aw_ae"]
    nrf = sb["aw_nrf"]
    wm = sb["c_wmask"]
    res_toks = []
    wslot_tok = {}
    guard = {m: None for m in range(8)}

    def wload(name, c, slot, extra=()):
        tokp = wslot_tok.get(slot)
        return C.load(wbuf[:, slot * 1024:(slot + 1) * 1024],
                      C.hbm[name][c * 128:(c + 1) * 128, :], cast=True,
                      waits=([tokp] if tokp else []) + list(extra),
                      key=f"wb{slot}")

    tva0 = p.tick("vector", lambda v: v.memset(va[:, :], 1.0),
                  waits=[xh_toks[-1]])

    for q in range(NQ):
        qs = slice(q * 512, (q + 1) * 512)
        mm_toks = {}
        for ci in range(NCH):
            lw = wload("aw_wv", ci, ci % 2, extra=wtoks if (q == 0 and ci == 0) else ())
            for bi in range(8):
                tt, vh = bi // 2, bi % 2
                w = [lw]
                if ci == 0:
                    w.append(xh_toks[q])
                    if q == 0:
                        w.append(tva0)
                    if guard[bi] is not None:
                        w.append(guard[bi])
                    elif q == 0:
                        w.append(xh_toks[bi // 2])
                fn = (lambda t, o=C.ps[bi][:, :], ci=ci, tt=tt, vh=vh, q=q,
                      st=(ci == 0), sp=(ci == 7):
                      t.matmul(out=o,
                               lhsT=XHc(C, ci)[:, q * 512 + tt * 128: q * 512 + (tt + 1) * 128],
                               rhs=wbuf[:, (ci % 2) * 1024 + vh * 512:(ci % 2) * 1024 + vh * 512 + 512],
                               start=st, stop=sp))
                tokm = p.tick("tensor", fn, w) if (bi == 7 or ci == 7) else p.emit("tensor", fn, w)
                if bi == 7:
                    wslot_tok[ci % 2] = tokm
                if ci == 7:
                    mm_toks[bi] = tokm
        for bi in range(8):
            tt, vh = bi // 2, bi % 2
            tt_g = q * 4 + tt
            dst = va[:, tt_g * 1040 + vh * 520: tt_g * 1040 + vh * 520 + 520]
            dst = dst.rearrange("p (h c) -> p h c", c=65)[:, :, 0:64]
            src = C.ps[bi][:, :].rearrange("p (h c) -> p h c", c=64)
            evt = p.tick("vector", lambda v, dst=dst, src=src: v.tensor_copy(
                out=dst, in_=src), waits=[mm_toks[bi]])
            guard[bi] = evt

    def proj_fm(dst_cols, q, wname):
        mm_last = {}
        for ci in range(NCH):
            slot = ci % 2
            lw = wload(wname, ci, slot)
            for m in range(8):
                w = [lw]
                if ci == 0:
                    w.append(xh_toks[q])
                    if guard[m] is not None:
                        w.append(guard[m])
                fn = (lambda t, o=C.ps[m][:, :], ci=ci, m=m, slot=slot, q=q,
                      st=(ci == 0), sp=(ci == 7):
                      t.matmul(out=o,
                               lhsT=wbuf[:, slot * 1024 + m * 128: slot * 1024 + (m + 1) * 128],
                               rhs=XHc(C, ci)[:, q * 512:(q + 1) * 512],
                               start=st, stop=sp))
                if m == 7 or ci == 7:
                    tokm = p.tick("tensor", fn, w)
                    if m == 7:
                        wslot_tok[slot] = tokm
                    if ci == 7:
                        mm_last[m] = tokm
                else:
                    p.emit("tensor", fn, w)
        last = None
        for m in range(8):
            last = p.tick("scalar", lambda s, m=m, dst=dst_cols(m): s.activation(
                out=dst, in_=C.ps[m][:, :], func=AF.Identity), waits=[mm_last[m]])
            guard[m] = last
        return last

    for q in range(NQ):
        def kdst(m, q=q):
            return kt_sb[:, m * T + q * 512: m * T + q * 512 + 512]
        proj_fm(kdst, q, "aw_wk")

    pt_guard = {0: None, 1: None}
    ae_guard = {0: None, 1: None}
    att_guard = None
    for qq in range(4):
        def qdst(m):
            return qt_sb[:, m * 512:(m + 1) * 512]
        qlast = proj_fm(qdst, qq, "aw_wq")
        apool = Banks([0, 1, 2, 3, 4, 5])
        for i in apool.ids:
            apool.free[i] = guard[i]
        abk_guard = {6: guard[6], 7: guard[7]}
        LAG = 3
        iters = [(h, tl) for h in range(16) for tl in range(4)]
        st_k = {}
        for kk in range(len(iters) + LAG):
            if kk < len(iters):
                h, tl = iters[kk]
                r0 = 64 * (h % 2)
                mcol = h // 2
                qt_g = qq * 4 + tl
                kts = [k for k in (qt_g - 1, qt_g, qt_g + 1) if 0 <= k < 16]
                r0k = kts[0] - (qt_g - 1)
                L = len(kts)
                psl = C.pcnt % 4
                C.pcnt += 1
                bi, bw = apool.acquire()
                stoks = None
                for j, ktg in enumerate(kts):
                    w = list(bw) if j == 0 else []
                    if j == 0 and h == 0 and tl == 0:
                        w.append(qlast)
                    stoks = p.tick("tensor", lambda t, bi=bi, j=j, ktg=ktg, r0=r0,
                                   mcol=mcol, tl=tl: t.matmul(
                        out=C.ps[bi][:, j * 128:(j + 1) * 128],
                        lhsT=kt_sb[r0:r0 + 64, mcol * T + ktg * 128: mcol * T + (ktg + 1) * 128],
                        rhs=qt_sb[r0:r0 + 64, mcol * 512 + tl * 128: mcol * 512 + (tl + 1) * 128],
                        start=True, stop=True), waits=w)
                pw = [stoks] + ([pt_guard[psl]] if pt_guard.get(psl) else [])
                tp = p.tick("scalar", lambda s, bi=bi, psl=psl, L=L: s.activation(
                    out=ptb[:, psl * 384: psl * 384 + L * 128],
                    in_=C.ps[bi][:, 0:L * 128], func=AF.Exp, scale=0.125), waits=pw)
                apool.release(bi, tp)
                tmsk = p.tick("vector", lambda v, psl=psl, L=L, r0k=r0k: v.tensor_tensor(
                    out=ptb[:, psl * 384: psl * 384 + L * 128],
                    in0=ptb[:, psl * 384: psl * 384 + L * 128],
                    in1=wm[:, r0k * 128:(r0k + L) * 128], op=ALU.mult), waits=[tp])
                st_k[kk] = (tmsk, psl, kts, L)
            if kk >= LAG:
                h, tl = iters[kk - LAG]
                r0 = 64 * (h % 2)
                mcol = h // 2
                slot = tl % 2
                tmsk, psl, kts, L = st_k.pop(kk - LAG)
                abk = 6 + (tl % 2)
                aw_ = [abk_guard[abk]] if abk_guard[abk] else []
                pv_tok = None
                for j, ktg in enumerate(kts):
                    pv_tok = p.tick("tensor", lambda t, abk=abk, j=j, ktg=ktg, h=h,
                                    psl=psl, st=(j == 0), sp=(j == L - 1): t.matmul(
                        out=C.ps[abk][0:65, 0:128],
                        lhsT=va[:, ktg * 1040 + h * 65: ktg * 1040 + (h + 1) * 65],
                        rhs=ptb[:, psl * 384 + j * 128: psl * 384 + (j + 1) * 128],
                        start=st, stop=sp), waits=[tmsk] + (aw_ if j == 0 else []))
                pt_guard[psl] = pv_tok
                tn1 = p.tick("vector", lambda v, abk=abk, slot=slot: v.reciprocal(
                    out=nrf[0:1, slot * 128:(slot + 1) * 128],
                    in_=C.ps[abk][64:65, 0:128]), waits=[pv_tok])
                tev = p.tick("vector", lambda v, abk=abk, slot=slot: v.tensor_copy(
                    out=att_e[:, slot * 128:(slot + 1) * 128],
                    in_=C.ps[abk][0:64, 0:128]),
                    waits=([ae_guard[slot]] if ae_guard[slot] else []))
                bj, bjw = apool.acquire()
                tb = p.tick("tensor", lambda t, bj=bj, slot=slot: t.matmul(
                    out=C.ps[bj][0:64, 0:128], lhsT=sb["c_ones_row_f"][0:1, 0:64],
                    rhs=nrf[0:1, slot * 128:(slot + 1) * 128],
                    start=True, stop=True), waits=bjw + [tn1])
                ta = p.tick("vector", lambda v, bj=bj, slot=slot, r0=r0, mcol=mcol,
                            tl=tl: v.tensor_tensor(
                    out=qt_sb[r0:r0 + 64, mcol * 512 + tl * 128: mcol * 512 + (tl + 1) * 128],
                    in0=att_e[:, slot * 128:(slot + 1) * 128],
                    in1=C.ps[bj][0:64, 0:128], op=ALU.mult), waits=[tb, tev])
                apool.release(bj, ta)
                ae_guard[slot] = ta
                abk_guard[abk] = ta
                att_guard = ta
        for i in apool.ids:
            guard[i] = apool.free[i]
        guard[6], guard[7] = abk_guard[6], abk_guard[7]
        omm = {}
        for ci in range(NCH):
            slot = ci % 2
            lw = wload("aw_wo", ci, slot)
            for m in range(8):
                w = [lw]
                if ci == 0:
                    w.append(att_guard)
                    if guard[m] is not None:
                        w.append(guard[m])
                fn = (lambda t, o=C.ps[m][:, :], ci=ci, m=m, slot=slot,
                      st=(ci == 0), sp=(ci == 7):
                      t.matmul(out=o,
                               lhsT=wbuf[:, slot * 1024 + m * 128: slot * 1024 + (m + 1) * 128],
                               rhs=qt_sb[:, ci * 512:(ci + 1) * 512],
                               start=st, stop=sp))
                if ci == 7 or m == 7:
                    tokm = p.tick("tensor", fn, w)
                    if m == 7:
                        wslot_tok[slot] = tokm
                    if ci == 7:
                        omm[m] = tokm
                else:
                    p.emit("tensor", fn, w)
        for m in range(8):
            gcol = qq * 512
            tx = p.tick("vector", lambda v, m=m, gcol=gcol: v.tensor_tensor(
                out=XRc(C, m)[:, gcol:gcol + 512], in0=XRc(C, m)[:, gcol:gcol + 512],
                in1=C.ps[m][:, :], op=ALU.add), waits=[omm[m]])
            guard[m] = tx
            res_toks.append(tx)
    return res_toks


MOE_DEBUG = "full"


def emit_moe(C, xh_toks, wtoks):
    p = C.p
    sb = C.sb
    gw = sb["mo_gw"]
    ew = sb["mo_ew"]
    eb = sb["mo_eb"]
    acc = sb["mo_acc"]
    coef = sb["mo_coef"]
    wsb = sb["mo_w"]
    zcol = sb["mo_z"]
    tbf = sb["mo_t"]
    abf = sb["mo_abf"]
    res_toks = []
    gpool = Banks([0, 1, 2, 3])
    gtoks = {}
    for pt in range(8):
        tts = (2 * pt, 2 * pt + 1)
        te_ = {}
        for sl, tt in enumerate(tts):
            bi, bw = gpool.acquire()
            bank = C.ps[bi]
            w = list(bw) + ([xh_toks[-1]] + list(wtoks) if tt == 0 else [])
            tm = C.mm_acc(bank[:, 0:16],
                          [(XHc(C, c)[:, tt * 128:(tt + 1) * 128],
                            gw[:, c * 16:(c + 1) * 16]) for c in range(NCH)],
                          waits=w)
            te_[sl] = p.tick("scalar", lambda s, bank=bank, sl=sl: s.activation(
                out=wsb[:, sl * 16:(sl + 1) * 16], in_=bank[:, 0:16], func=AF.Exp,
                accum_out=zcol[:, sl:sl + 1]),
                waits=[tm] + ([gtoks[tt - 2]] if tt >= 2 else []))
            gpool.release(bi, te_[sl])
        for sl in (0, 1):
            p.tick("vector", lambda v, sl=sl: v.reciprocal(
                out=zcol[:, 2 + sl:3 + sl], in_=zcol[:, sl:sl + 1]),
                waits=[te_[sl]])
        for sl in (0, 1):
            p.tick("vector", lambda v, sl=sl: v.reduce_max(
                out=zcol[:, 4 + sl:5 + sl], in_=wsb[:, sl * 16:(sl + 1) * 16],
                axis=AX.X))
        for sl in (0, 1):
            p.tick("vector", lambda v, sl=sl: v.tensor_scalar(
                out=sb["mo_sc"][:, sl * 16:(sl + 1) * 16],
                in0=wsb[:, sl * 16:(sl + 1) * 16],
                scalar1=zcol[:, 4 + sl:5 + sl], scalar2=None, op0=ALU.is_ge))
        for sl in (0, 1):
            p.tick("vector", lambda v, sl=sl: v.tensor_scalar(
                out=sb["mo_sc"][:, sl * 16:(sl + 1) * 16],
                in0=sb["mo_sc"][:, sl * 16:(sl + 1) * 16],
                scalar1=-1.0, scalar2=1.0, op0=ALU.mult, op1=ALU.add))
        for sl in (0, 1):
            p.tick("vector", lambda v, sl=sl: v.tensor_tensor(
                out=sb["mo_sc"][:, 32 + sl * 16:32 + (sl + 1) * 16],
                in0=wsb[:, sl * 16:(sl + 1) * 16],
                in1=sb["mo_sc"][:, sl * 16:(sl + 1) * 16], op=ALU.mult))
        for sl in (0, 1):
            p.tick("vector", lambda v, sl=sl: v.reduce_max(
                out=zcol[:, 6 + sl:7 + sl],
                in_=sb["mo_sc"][:, 32 + sl * 16:32 + (sl + 1) * 16], axis=AX.X))
        for sl in (0, 1):
            p.tick("vector", lambda v, sl=sl: v.tensor_scalar(
                out=sb["mo_sc"][:, sl * 16:(sl + 1) * 16],
                in0=wsb[:, sl * 16:(sl + 1) * 16],
                scalar1=zcol[:, 6 + sl:7 + sl], scalar2=None, op0=ALU.is_ge))
        for sl in (0, 1):
            p.tick("vector", lambda v, sl=sl: v.tensor_tensor(
                out=sb["mo_sc"][:, sl * 16:(sl + 1) * 16],
                in0=wsb[:, sl * 16:(sl + 1) * 16],
                in1=sb["mo_sc"][:, sl * 16:(sl + 1) * 16], op=ALU.mult))
        for sl, tt in enumerate(tts):
            gtoks[tt] = p.tick("vector", lambda v, sl=sl, tt=tt: v.tensor_scalar(
                out=coef[:, tt * 16:(tt + 1) * 16],
                in0=sb["mo_sc"][:, sl * 16:(sl + 1) * 16],
                scalar1=zcol[:, 2 + sl:3 + sl], scalar2=None, op0=ALU.mult))
    if MOE_DEBUG == "gate":
        return [gtoks[15]]
    epool = Banks([0, 1, 2, 3])
    ew_guard = {}
    acc_tok = {}
    for grp_e in [(g, e) for g in range(4) for e in range(16)]:
        grp, e = grp_e
        esl = e % 2
        ltoks = []
        for c in range(NCH):
            ltoks.append(C.load(ew[:, esl * 8192 + c * 1024: esl * 8192 + (c + 1) * 1024],
                                C.hbm["mo_ew"][e * 1024 + c * 128: e * 1024 + (c + 1) * 128, :],
                                cast=True,
                                waits=[ew_guard[esl]] if (c == 0 and esl in ew_guard) else (),
                                key=f"ew{esl}"))
        ltoks.append(C.load(eb[:, esl * 1024:(esl + 1) * 1024],
                            C.hbm["mo_eb"][e:e + 1, :], cast=True, key=f"ew{esl}"))
        last_mm = None
        for tt in range(grp * 4, grp * 4 + 4):
            for vh in range(2):
                bi, bw = epool.acquire()
                bank = C.ps[bi]
                w = list(bw) + ([ltoks[-1], ltoks[-2]]
                                if (tt == grp * 4 and vh == 0) else [])
                if e == 0 and tt == 0:
                    w.append(gtoks[15])
                pairs = [(XHc(C, c)[:, tt * 128:(tt + 1) * 128],
                          ew[:, esl * 8192 + c * 1024 + vh * 512:
                             esl * 8192 + c * 1024 + vh * 512 + 512])
                         for c in range(NCH)]
                pairs.append((sb["c_ones_row"][:, :],
                              eb[:, esl * 1024 + vh * 512: esl * 1024 + vh * 512 + 512]))
                tm = C.mm_acc(bank[:, :], pairs, waits=w)
                last_mm = tm
                slot = (tt * 2 + vh) % 2
                if e == 0:
                    ev = p.tick("scalar", lambda s, bank=bank, tl=(tt % 4), tg=tt,
                                vh=vh, e=e:
                                s.activation(out=acc[:, tl * 1024 + vh * 512:
                                                     tl * 1024 + vh * 512 + 512],
                                             in_=bank[:, :], func=AF.Relu,
                                             scale=coef[:, tg * 16 + e:tg * 16 + e + 1]),
                                waits=[tm])
                    epool.release(bi, ev)
                    acc_tok[(tt, vh)] = ev
                else:
                    ev = p.tick("scalar", lambda s, bank=bank, slot=slot, tg=tt, e=e:
                                s.activation(out=tbf[:, slot * 512:(slot + 1) * 512],
                                             in_=bank[:, :], func=AF.Relu,
                                             scale=coef[:, tg * 16 + e:tg * 16 + e + 1]),
                                waits=[tm] + ([acc_tok[("t", slot)]]
                                              if ("t", slot) in acc_tok else []))
                    epool.release(bi, ev)
                    ta = p.tick("vector", lambda v, slot=slot, tl=(tt % 4), vh=vh:
                                v.tensor_tensor(
                                    out=acc[:, tl * 1024 + vh * 512: tl * 1024 + vh * 512 + 512],
                                    in0=acc[:, tl * 1024 + vh * 512: tl * 1024 + vh * 512 + 512],
                                    in1=tbf[:, slot * 512:(slot + 1) * 512],
                                    op=ALU.add), waits=[ev, acc_tok[(tt, vh)]])
                    acc_tok[(tt, vh)] = ta
                    acc_tok[("t", slot)] = ta
        ew_guard[esl] = last_mm
        if MOE_DEBUG == "expert":
            continue
        if e == 15:
            tp_guard = acc_tok.setdefault("tpg", {0: None, 1: None})
            for tl in range(4):
                tt = grp * 4 + tl
                tcp = p.tick("scalar", lambda s, tl=tl: s.copy(
                    out=abf[:, :], in_=acc[:, tl * 1024:(tl + 1) * 1024]),
                    waits=[acc_tok[(tt, 0)], acc_tok[(tt, 1)],
                           acc_tok.get(("abf",), p.tok("vector"))])
                for c in range(NCH):
                    pT = C.ps[4 + c % 2]
                    w = [tcp] + ([tp_guard[c % 2]] if tp_guard[c % 2] else [])
                    tr = p.tick("tensor", lambda t, c=c, pT=pT: t.matmul(
                        out=pT[:, 0:128], lhsT=abf[:, c * 128:(c + 1) * 128],
                        rhs=sb["c_ident"][:, :], start=True, stop=True), waits=w)
                    tx = p.tick("vector", lambda v, c=c, tt=tt, pT=pT: v.tensor_tensor(
                        out=XRc(C, c)[:, tt * 128:(tt + 1) * 128],
                        in0=XRc(C, c)[:, tt * 128:(tt + 1) * 128],
                        in1=pT[:, 0:128], op=ALU.add), waits=[tr])
                    tp_guard[c % 2] = tx
                    res_toks.append(tx)
                acc_tok[("abf",)] = res_toks[-1]
            for tl in range(4):
                acc_tok[(grp * 4 + tl, 0)] = res_toks[-1]
                acc_tok[(grp * 4 + tl, 1)] = res_toks[-1]
    if MOE_DEBUG == "expert":
        return [acc_tok[(15, 1)]]
    return res_toks


MB_DEBUG = "full"
MB_ZEROCARRY = False


def emit_mamba(C, ln_toks, r2_toks, wtoks):
    p = C.p
    sb = C.sb
    TH = 512
    ab = sb["ab_ab"]
    r3 = sb["ab_r2"]
    xv = sb["ab_xv"]
    ZS = sb["mb_zs"]
    XC = sb["mb_xc"]
    CV = sb["mb_cv"]
    XTL = sb["mb_xtl"]
    DLq = sb["mb_dl"]
    UQ = sb["mb_u"]
    BQ = sb["mb_bq"]
    CQ = sb["mb_cq"]
    YB = sb["mb_y"]
    HT = sb["mb_h"]
    TC = sb["mb_tc"]
    AF32 = sb["mb_a"]
    BF_ = sb["mb_b"]
    HC = sb["mb_hc"]
    SPS = sb["mb_sp"]
    WS = sb["mb_ws"]
    BR = sb["mb_br"]
    negA = sb["mb_nega"]
    guard = {m: None for m in range(8)}
    wslot_tok = {}

    def wload(name, row0, slot, extra=()):
        slot = slot % 2
        tokp = wslot_tok.get(slot)
        return C.load(WS[:, slot * 2048: slot * 2048 + C.hbm[name].shape[1]],
                      C.hbm[name][row0:row0 + 128, :], cast=True,
                      waits=([tokp] if tokp else []) + list(extra),
                      key=f"ws{slot}")

    tg0 = p.tick("scalar", lambda s: s.activation(out=negA[:, :],
                                                  in_=sb["mb_alog"][:, :],
                                                  func=AF.Exp), waits=wtoks)
    p.tick("vector", lambda v: v.tensor_scalar(
        out=negA[:, :], in0=negA[:, :], scalar1=-1.0, scalar2=None,
        op0=ALU.mult), waits=[tg0])
    hc0 = p.tick("vector", lambda v: v.memset(HC[:, :], 0.0), waits=wtoks)

    def xhat_tiles(gq, extra):
        cs = slice(gq * 512, (gq + 1) * 512)
        xtok = None
        for c in range(NCH):
            p.emit("vector", lambda v, c=c, cs=cs: v.tensor_tensor(
                out=sb["ab_xt"][:, :], in0=XRc(C, c)[:, cs],
                in1=ab[:, 0 * T:1 * T][:, cs], op=ALU.mult),
                waits=extra if c == 0 else ())
            p.emit("vector", lambda v, cs=cs: v.tensor_tensor(
                out=sb["ab_xt"][:, :], in0=sb["ab_xt"][:, :],
                in1=ab[:, 1 * T:2 * T][:, cs], op=ALU.add))
            p.emit("vector", lambda v, cs=cs: v.tensor_tensor(
                out=sb["ab_xt"][:, :], in0=sb["ab_xt"][:, :],
                in1=r3[:, cs], op=ALU.mult))
            xtok = p.tick("vector", lambda v, c=c: v.tensor_scalar(
                out=xv[:, c * 512:(c + 1) * 512], in0=sb["ab_xt"][:, :],
                scalar1=sb["mb_gam"][:, c:c + 1], scalar2=sb["mb_bet"][:, c:c + 1],
                op0=ALU.mult, op1=ALU.add))
        return xtok

    res_toks = []
    res_inner = None
    slot_g = {}
    bq_guard = {}
    hc_tok = {}
    for th in range(4):
        if MB_DEBUG == "hc1" and th > 0:
            continue
        zguard = {}
        xp_toks = {}
        last_tm = None
        for lq in range(1):
            gq = th
            xw = [ln_toks[gq], r2_toks[gq]]
            if th == 0 and lq == 0:
                xw.append(r2_toks[-1])
            if res_toks:
                xw.append(res_toks[-1])
            if last_tm is not None:
                xw.append(last_tm)
            xtok = xhat_tiles(gq, xw)
            for mi in range(32):
                slot = mi % 2
                lw = wload("mb_win", mi * 128, slot,
                           extra=wtoks if (th == 0 and lq == 0 and mi == 0) else ())
                bank = C.ps[mi % 4]
                w = [lw, xtok]
                if guard[mi % 4] is not None:
                    w.append(guard[mi % 4])
                tm = C.mm_acc(bank[:, :],
                              [(WS[:, (mi % 2) * 2048 + c * 128:
                                   (mi % 2) * 2048 + (c + 1) * 128],
                                xv[:, c * 512:(c + 1) * 512]) for c in range(NCH)],
                              waits=w)
                wslot_tok[mi % 2] = tm
                last_tm = tm
                if mi < 16:
                    ev = p.tick("vector", lambda v, bank=bank, mi=mi, lq=lq:
                                v.tensor_copy(
                        out=XC[:, mi * TH + lq * 512: mi * TH + lq * 512 + 512],
                        in_=bank[:, :]), waits=[tm])
                    xp_toks[(mi, lq)] = ev
                else:
                    ev = p.tick("scalar", lambda s, bank=bank, mi=mi, lq=lq:
                                s.activation(
                        out=ZS[:, (mi - 16) * TH + lq * 512:
                               (mi - 16) * TH + lq * 512 + 512],
                        in_=bank[:, :], func=AF.Silu), waits=[tm])
                    zguard[(mi - 16, lq)] = ev
                guard[mi % 4] = ev
        for c in range(NCH):
            for lq in range(1):
                gq2 = th
                cs = slice(gq2 * 512, (gq2 + 1) * 512)
                p.emit("vector", lambda v, c=c, cs=cs: v.tensor_tensor(
                    out=sb["ab_xt"][:, :], in0=XRc(C, c)[:, cs],
                    in1=ab[:, 0 * T:1 * T][:, cs], op=ALU.mult))
                p.emit("vector", lambda v, cs=cs: v.tensor_tensor(
                    out=sb["ab_xt"][:, :], in0=sb["ab_xt"][:, :],
                    in1=ab[:, 1 * T:2 * T][:, cs], op=ALU.add))
                res_inner = p.tick("vector", lambda v, c=c, cs=cs: v.tensor_tensor(
                    out=XRc(C, c)[:, cs], in0=XRc(C, c)[:, cs],
                    in1=sb["ab_xt"][:, :], op=ALU.add))
        res_toks.append(res_inner)
        if MB_DEBUG == "phaseB":
            continue
        conv_toks = {}
        for dt in range(16):
            w0 = [xp_toks[(dt, 0)]]
            tl = p.tick("vector", lambda v, dt=dt, dst=sb["mb_xtl2"]: v.tensor_copy(
                out=dst[:, 3 * dt:3 * dt + 3],
                in_=XC[:, dt * TH + TH - 3: dt * TH + TH]),
                waits=w0 + ([conv_toks[dt - 1]] if dt > 0 else []))
            t0 = p.tick("vector", lambda v, dt=dt: v.tensor_scalar(
                out=CV[:, :], in0=XC[:, dt * TH:(dt + 1) * TH],
                scalar1=sb["mb_wcv"][:, 4 * dt + 3:4 * dt + 4],
                scalar2=sb["mb_bcv"][:, dt:dt + 1], op0=ALU.mult, op1=ALU.add),
                waits=[r2_toks[-1]] if (th == 0 and dt == 0) else ())
            for jj, sh in ((2, 1), (1, 2), (0, 3)):
                t0 = p.tick("vector", lambda v, dt=dt, jj=jj, sh=sh:
                            v.scalar_tensor_tensor(
                    out=CV[:, sh:TH], in0=XC[:, dt * TH: dt * TH + TH - sh],
                    scalar=sb["mb_wcv"][:, 4 * dt + jj:4 * dt + jj + 1],
                    in1=CV[:, sh:TH], op0=ALU.mult, op1=ALU.add))
                if th >= 1:
                    for tcol in range(sh):
                        t0 = p.tick("vector", lambda v, dt=dt, jj=jj, sh=sh,
                                    tcol=tcol, XT=XTL: v.scalar_tensor_tensor(
                            out=CV[:, tcol:tcol + 1],
                            in0=XT[:, 3 * dt + 3 - sh + tcol:
                                   3 * dt + 4 - sh + tcol],
                            scalar=sb["mb_wcv"][:, 4 * dt + jj:4 * dt + jj + 1],
                            in1=CV[:, tcol:tcol + 1], op0=ALU.mult, op1=ALU.add))
            conv_toks[dt] = p.tick("scalar", lambda s, dt=dt: s.activation(
                out=XC[:, dt * TH:(dt + 1) * TH], in_=CV[:, :], func=AF.Silu),
                waits=[t0])
        if MB_DEBUG == "conv":
            res_toks.append(conv_toks[15])
            continue
        sb["mb_xtl2"], sb["mb_xtl"] = sb["mb_xtl"], sb["mb_xtl2"]
        XTL = sb["mb_xtl"]
        row_toks = []
        for which in (0, 1):
            wname = "mb_wb" if which == 0 else "mb_wc"
            for lq in range(1):
                bank = C.ps[4 + which]
                w = [conv_toks[15]] if lq == 0 else []
                if guard.get(4 + which) is not None:
                    w.append(guard[4 + which])
                tm = C.mm_acc(bank[0:16, :],
                              [(sb[wname][:, k * 16:(k + 1) * 16],
                                XC[:, k * TH + lq * 512: k * TH + lq * 512 + 512])
                               for k in range(16)], waits=w)
                rt = p.tick("vector", lambda v, bank=bank, which=which, lq=lq:
                            v.tensor_copy(
                    out=BR[0:16, which * TH + lq * 512: which * TH + lq * 512 + 512],
                    in_=bank[0:16, :]), waits=[tm])
                row_toks.append(rt)
                guard[4 + which] = rt
        if MB_DEBUG == "rows":
            res_toks.append(row_toks[-1])
            continue
        for lq in range(1):
            gq = th
            pend_hc = None
            if MB_DEBUG == "scan1" and (th, lq) != (0, 0):
                continue
            if MB_DEBUG == "scan2" and th != 0:
                continue
            if MB_DEBUG == "scan12" and th >= 2:
                continue
            if MB_DEBUG == "scan3" and (th, lq) == (1, 1):
                continue
            for dg in range(4):
                dl_toks = {}
                for di in range(4):
                    dt = dg * 4 + di
                    slot = dt % 2
                    lw = wload("mb_wdel", dt * 128, slot)
                    bank = C.ps[di % 2]
                    w = [lw]
                    if lq == 0 and dg == 0 and di < 2:
                        w.append(row_toks[-1])
                    if guard.get(di % 2) is not None:
                        w.append(guard[di % 2])
                    tm = C.mm_acc(bank[:, :],
                                  [(WS[:, (dt % 2) * 2048 + k * 128:
                                       (dt % 2) * 2048 + (k + 1) * 128],
                                    XC[:, k * TH + lq * 512: k * TH + lq * 512 + 512])
                                   for k in range(16)], waits=w)
                    wslot_tok[dt % 2] = tm
                    s1 = p.tick("scalar", lambda s, bank=bank, dt=dt: s.activation(
                        out=SPS[:, :], in_=bank[:, :], func=AF.Abs,
                        bias=sb["mb_bdel"][:, dt:dt + 1]),
                        waits=[tm] + ([slot_g["sps"]] if "sps" in slot_g else []))
                    s2 = p.tick("scalar", lambda s: s.activation(
                        out=SPS[:, :], in_=SPS[:, :], func=AF.Exp, scale=-1.0))
                    s3 = p.tick("scalar", lambda s: s.activation(
                        out=SPS[:, :], in_=SPS[:, :], func=AF.Ln,
                        bias=sb["cpool"][:, 3:4]))
                    s4 = p.tick("scalar", lambda s, bank=bank, di=di, dt=dt:
                                s.activation(
                        out=DLq[:, di * 512:(di + 1) * 512], in_=bank[:, :],
                        func=AF.Relu, bias=sb["mb_bdel"][:, dt:dt + 1]), waits=[s3])
                    s5 = p.tick("vector", lambda v, di=di: v.tensor_tensor(
                        out=DLq[:, di * 512:(di + 1) * 512],
                        in0=DLq[:, di * 512:(di + 1) * 512], in1=SPS[:, :],
                        op=ALU.add), waits=[s4])
                    slot_g["sps"] = s5
                    guard[di % 2] = s5
                    ut = p.tick("gpsimd", lambda g, di=di, dt=dt, lq=lq:
                                g.tensor_tensor(
                        out=UQ[:, di * 512:(di + 1) * 512],
                        in0=DLq[:, di * 512:(di + 1) * 512],
                        in1=XC[:, dt * TH + lq * 512: dt * TH + lq * 512 + 512],
                        op=ALU.mult), waits=[s5])
                    dl_toks[di] = ut
                for nh in range(4):
                    for ni in range(4):
                        n = nh * 4 + ni
                        for which, cache in ((0, BQ), (1, CQ)):
                            bank = C.ps[2 + (ni % 2)]
                            w = []
                            if (ni % 2, which) in bq_guard:
                                w.append(bq_guard[(ni % 2, which)])
                            if guard.get(2 + (ni % 2)) is not None:
                                w.append(guard[2 + (ni % 2)])
                            tm = p.tick("tensor", lambda t, bank=bank, n=n,
                                        which=which, lq=lq: t.matmul(
                                out=bank[:, :],
                                lhsT=sb["c_sel16"][:, n * 128:(n + 1) * 128],
                                rhs=BR[0:16, which * TH + lq * 512:
                                       which * TH + lq * 512 + 512],
                                start=True, stop=True), waits=w)
                            ev = p.tick("scalar", lambda s, bank=bank, ni=ni,
                                        cache=cache: s.copy(
                                out=cache[:, ni * 512:(ni + 1) * 512],
                                in_=bank[:, :]), waits=[tm])
                            bq_guard[(ni % 2, which)] = ev
                            guard[2 + (ni % 2)] = ev
                    for di in range(4):
                        dt = dg * 4 + di
                        ybank = 4 + di
                        for ni in range(4):
                            n = nh * 4 + ni
                            sl = n % 6
                            sl2 = n % 4
                            aw_ = [dl_toks[di]] if n == 0 else []
                            a1 = p.tick("scalar", lambda s, di=di, dt=dt, n=n,
                                        sl2=sl2: s.activation(
                                out=AF32[:, sl2 * 512:(sl2 + 1) * 512],
                                in_=DLq[:, di * 512:(di + 1) * 512], func=AF.Exp,
                                scale=negA[:, dt * 16 + n:dt * 16 + n + 1]),
                                waits=aw_ + ([slot_g[("a", sl2)]]
                                             if ("a", sl2) in slot_g else []))
                            bsl = n % 4
                            b1 = p.tick("vector", lambda v, di=di, ni=ni,
                                        bsl=bsl: v.tensor_tensor(
                                out=BF_[:, bsl * 512:(bsl + 1) * 512],
                                in0=UQ[:, di * 512:(di + 1) * 512],
                                in1=BQ[:, ni * 512:(ni + 1) * 512], op=ALU.mult),
                                waits=([bq_guard[(ni % 2, 0)]] if di == 0 else [])
                                      + ([dl_toks[di]] if n == 0 else []))
                            hcol = dt * 16 + n
                            init_ap = (sb["cpool"][:, 4:5]
                                       if (MB_ZEROCARRY and th > 0)
                                       else HC[:, hcol:hcol + 1])
                            if pend_hc is not None:
                                psl, phcol, psc = pend_hc
                                hct = p.tick("gpsimd", lambda g, psl=psl,
                                             phcol=phcol: g.tensor_copy(
                                    out=HC[:, phcol:phcol + 1],
                                    in_=HT[:, psl * 512 + 511: psl * 512 + 512]),
                                    waits=[psc])
                                slot_g[("hc", psl)] = hct
                                hc_tok[phcol] = hct
                            sc = p.tick("vector", lambda v, sl=sl, sl2=sl2,
                                        bsl=bsl, init_ap=init_ap:
                                        v.tensor_tensor_scan(
                                out=HT[:, sl * 512:(sl + 1) * 512],
                                data0=AF32[:, sl2 * 512:(sl2 + 1) * 512],
                                data1=BF_[:, bsl * 512:(bsl + 1) * 512],
                                initial=init_ap,
                                op0=ALU.mult, op1=ALU.add),
                                waits=[a1]
                                      + ([slot_g[("hc", sl)]]
                                         if ("hc", sl) in slot_g else [])
                                      + ([hc_tok[hcol]] if hcol in hc_tok else [])
                                      + ([hc0] if (th == 0 and lq == 0 and dg == 0
                                                   and n == 0) else []))
                            pend_hc = (sl, hcol, sc)
                            tmul = p.tick("vector", lambda v, sl=sl, ni=ni:
                                          v.tensor_tensor(
                                out=TC[:, sl * 512:(sl + 1) * 512],
                                in0=HT[:, sl * 512:(sl + 1) * 512],
                                in1=CQ[:, ni * 512:(ni + 1) * 512], op=ALU.mult),
                                waits=([bq_guard[(ni % 2, 1)]] if di == 0 else [])
                                      + ([slot_g[("mm", sl)]]
                                         if ("mm", sl) in slot_g else []))
                            slot_g[("a", sl2)] = tmul
                            slot_g[("t", sl)] = tmul
                            if ni >= 2:
                                self_n = n - 2
                                slp = self_n % 6
                                ylw = [slot_g[("t", slp)]]
                                if self_n == 0 and guard.get(ybank) is not None:
                                    ylw.append(guard[ybank])
                                ym = p.tick("tensor", lambda t, ybank=ybank,
                                            slp=slp, st=(self_n == 0): t.matmul(
                                    out=C.ps[ybank][:, :],
                                    lhsT=sb["c_ident"][:, :],
                                    rhs=TC[:, slp * 512:(slp + 1) * 512],
                                    start=st, stop=False), waits=ylw)
                                slot_g[("mm", slp)] = ym
                        for self_n in (nh * 4 + 2, nh * 4 + 3):
                            slp = self_n % 6
                            ylw = [slot_g[("t", slp)]]
                            if self_n == 0 and guard.get(ybank) is not None:
                                ylw.append(guard[ybank])
                            ym = p.tick("tensor", lambda t, ybank=ybank, slp=slp,
                                        st=(self_n == 0), sp=(self_n == 15):
                                        t.matmul(
                                out=C.ps[ybank][:, :], lhsT=sb["c_ident"][:, :],
                                rhs=TC[:, slp * 512:(slp + 1) * 512],
                                start=st, stop=sp), waits=ylw)
                            slot_g[("mm", slp)] = ym
                            if self_n == 15:
                                ye = p.tick("vector", lambda v, dt=dt, ybank=ybank,
                                            lq=lq: v.scalar_tensor_tensor(
                                    out=YB[:, dt * 512:(dt + 1) * 512],
                                    in0=XC[:, dt * TH + lq * 512:
                                           dt * TH + lq * 512 + 512],
                                    scalar=sb["mb_dpar"][:, dt:dt + 1],
                                    in1=C.ps[ybank][:, :], op0=ALU.mult,
                                    op1=ALU.add), waits=[ym])
                                yz = p.tick("vector", lambda v, dt=dt, lq=lq:
                                            v.tensor_tensor(
                                    out=YB[:, dt * 512:(dt + 1) * 512],
                                    in0=YB[:, dt * 512:(dt + 1) * 512],
                                    in1=ZS[:, dt * TH + lq * 512:
                                           dt * TH + lq * 512 + 512],
                                    op=ALU.mult),
                                    waits=[zguard[(dt, lq)]]
                                          if (dt, lq) in zguard else ())
                                guard[ybank] = yz
            if pend_hc is not None:
                psl, phcol, psc = pend_hc
                hct = p.tick("gpsimd", lambda g, psl=psl, phcol=phcol:
                             g.tensor_copy(
                    out=HC[:, phcol:phcol + 1],
                    in_=HT[:, psl * 512 + 511: psl * 512 + 512]), waits=[psc])
                slot_g[("hc", psl)] = hct
                hc_tok[phcol] = hct
                pend_hc = None
            omm = {}
            for dt in range(16):
                slot = dt % 2
                lw = wload("mb_wout", dt * 128, slot)
                for m in range(8):
                    w = [lw]
                    if dt == 0:
                        w.append(guard[7])
                        if guard.get(m) is not None:
                            w.append(guard[m])
                    fn = (lambda t, o=C.ps[m][:, :], dt=dt, m=m, slot=slot,
                          st=(dt == 0), sp=(dt == 15):
                          t.matmul(out=o,
                                   lhsT=WS[:, (dt % 2) * 2048 + m * 128:
                                           (dt % 2) * 2048 + (m + 1) * 128],
                                   rhs=YB[:, dt * 512:(dt + 1) * 512],
                                   start=st, stop=sp))
                    if m == 7 or dt == 15:
                        tokm = p.tick("tensor", fn, w)
                        if m == 7:
                            wslot_tok[dt % 2] = tokm
                        if dt == 15:
                            omm[m] = tokm
                    else:
                        p.emit("tensor", fn, w)
            for m in range(8):
                tx = p.tick("vector", lambda v, m=m, gq=gq: v.tensor_tensor(
                    out=XRc(C, m)[:, gq * 512:(gq + 1) * 512],
                    in0=XRc(C, m)[:, gq * 512:(gq + 1) * 512],
                    in1=C.ps[m][:, :], op=ALU.add), waits=[omm[m]])
                guard[m] = tx
                res_toks.append(tx)
    if MB_DEBUG in ("hc", "hc1"):
        res_toks.append(p.tick("vector", lambda v: v.tensor_copy(
            out=sb["XR"][:, 0:256], in_=HC[:, :])))
    return res_toks


def emit_ffn(C, xh_toks):
    p = C.p
    sb = C.sb
    wgu = sb["f_wgub"]
    wd = sb["f_wdb"]
    h_sb = sb["f_h"]
    sg = sb["f_sg"]
    res_toks = []
    slot_tok = {}
    for H in range(2):
        hq_toks = {}
        for m in range(24):
            slot = m % 2
            wwait = [slot_tok[("gu", slot)]] if ("gu", slot) in slot_tok else []
            if H == 0 and m == 0:
                wwait = wwait + list(C.wtoks)
            lg = C.load(wgu[:, slot * 1024:(slot + 1) * 1024],
                        C.hbm["f_wg"][m * 128:(m + 1) * 128, :], cast=True,
                        waits=wwait, key=f"fg{slot}")
            lu = C.load(wgu[:, (2 + slot) * 1024:(3 + slot) * 1024],
                        C.hbm["f_wu"][m * 128:(m + 1) * 128, :], cast=True,
                        key=f"fu{slot}")
            for q2 in range(2):
                cs = slice(H * 1024 + q2 * 512, H * 1024 + q2 * 512 + 512)
                bg, bu = C.ps[q2 * 2], C.ps[q2 * 2 + 1]
                mwaits = [lg, lu]
                if m == 0:
                    mwaits.append(xh_toks[-1])
                    if res_toks:
                        mwaits.append(res_toks[-1])
                else:
                    mwaits.append(hq_toks[(m - 1, q2)])
                tg = C.mm_acc(bg[:, :], [(wgu[:, slot * 1024 + c * 128:
                                              slot * 1024 + (c + 1) * 128],
                                          XHc(C, c)[:, cs]) for c in range(NCH)],
                              waits=mwaits)
                tu = C.mm_acc(bu[:, :], [(wgu[:, (2 + slot) * 1024 + c * 128:
                                              (2 + slot) * 1024 + (c + 1) * 128],
                                          XHc(C, c)[:, cs]) for c in range(NCH)])
                ts_ = p.tick("scalar", lambda s, bg=bg, q2=q2: s.activation(
                    out=sg[:, q2 * 512:(q2 + 1) * 512], in_=bg[:, :],
                    func=AF.Silu, scale=S127), waits=[tg])
                th = p.tick("vector", lambda v, bu=bu, q2=q2, m=m: v.tensor_tensor(
                    out=h_sb[:, m * 1024 + q2 * 512: m * 1024 + q2 * 512 + 512],
                    in0=sg[:, q2 * 512:(q2 + 1) * 512], in1=bu[:, :], op=ALU.mult),
                    waits=[ts_, tu])
                hq_toks[(m, q2)] = th
                if q2 == 1:
                    slot_tok[("gu", slot)] = tu
        for grp in range(2):
            dn_toks = {}
            for m in range(24):
                slot = m % 2
                wwait = [slot_tok[("wd", slot)]] if ("wd", slot) in slot_tok else []
                ld = C.load(wd[:, slot * 1024:(slot + 1) * 1024],
                            C.hbm["f_wd"][m * 128:(m + 1) * 128, :], cast=True,
                            waits=wwait, key=f"fd{slot}")
                last = None
                for dm in range(4):
                    d = grp * 4 + dm
                    for q2 in range(2):
                        bank = C.ps[dm * 2 + q2]
                        if dm == 0:
                            w = [ld, hq_toks[(m, q2)]]
                            if m == 0:
                                w.append(hq_toks[(23, q2)])
                                if res_toks:
                                    w.append(res_toks[-1])
                        else:
                            w = []
                        fn = (lambda t, o=bank[:, :],
                              l=wd[:, slot * 1024 + d * 128: slot * 1024 + (d + 1) * 128],
                              r=h_sb[:, m * 1024 + q2 * 512: m * 1024 + q2 * 512 + 512],
                              st=(m == 0), sp=(m == 23):
                              t.matmul(out=o, lhsT=l, rhs=r, start=st, stop=sp))
                        if (dm == 3 and q2 == 1) or m == 23:
                            tok = p.tick("tensor", fn, w)
                            last = tok
                            if m == 23:
                                dn_toks[(d, q2)] = tok
                        else:
                            p.emit("tensor", fn, w)
                slot_tok[("wd", slot)] = last
            for dm in range(4):
                d = grp * 4 + dm
                for q2 in range(2):
                    bank = C.ps[dm * 2 + q2]
                    cs = slice(H * 1024 + q2 * 512, H * 1024 + q2 * 512 + 512)
                    tt = p.tick("vector", lambda v, bank=bank, d=d, cs=cs:
                                v.scalar_tensor_tensor(
                                    out=XRc(C, d)[:, cs], in0=bank[:, :],
                                    scalar=S127SQ, in1=XRc(C, d)[:, cs],
                                    op0=ALU.mult, op1=ALU.add),
                                waits=[dn_toks[(d, q2)]])
                    res_toks.append(tt)
    return res_toks




_LN_XH_BF = {"XH": (128, 0, 16384), "lnxb": (128, 16384, 20480),
             "lnsq": (128, 20480, 24576), "lnA": (128, 24576, 25088),
             "lnB": (128, 25088, 25600), "lntmp": (128, 25600, 26112)}
_LN_F3 = {"lnmu": (128, 0, 512), "lnmsq": (128, 512, 1024),
          "lnvar": (128, 1024, 1536)}

_CARVE_BF = {
    "aa0": dict(_LN_XH_BF, **{
        "aa_wb": (128, 26112, 28160),
        "aa_qt": (128, 28160, 32256), "aa_kt": (128, 32256, 36352),
        "aa_v": (128, 36352, 40448), "aa_pt": (128, 40448, 40704),
        "aa_bv": (1, 40704, 41728)}),
    "f0a": dict(_LN_XH_BF, **{
        "f_wgub": (128, 26112, 30208), "f_wdb": (128, 30208, 32256),
        "f_h": (128, 32256, 56832), "f_sg": (128, 56832, 57856)}),
    "ab0": {
        "ab_ab": (128, 0, 4096), "ab_r2": (128, 4096, 6144),
        "ab_xv": (128, 6144, 10240), "ab_xt": (128, 10240, 10752),
        "ab_w": (128, 10752, 12800), "ab_ro": (128, 12800, 16896),
        "ab_kt": (128, 16896, 33280), "ab_qt": (128, 33280, 37376),
        "ab_va": (128, 37376, 54016),
        "lnxb": (128, 37376, 41472), "lnsq": (128, 41472, 45568),
        "ab_pt": (128, 54016, 56064), "ab_qr": (128, 56064, 57088),
        "ab_ae": (64, 57088, 58112), "ab_bv": (1, 58112, 59136)},
    "aw1": {
        "XH": (128, 0, 16384), "lnA": (128, 16384, 16896),
        "lnB": (128, 16896, 17408), "lntmp": (128, 17408, 17920),
        "aw_w": (128, 17920, 19968),
        "aw_kt": (128, 19968, 36352), "aw_qt": (128, 36352, 40448),
        "aw_va": (128, 40448, 57088),
        "lnxb": (128, 40448, 44544), "lnsq": (128, 44544, 48640),
        "aw_pt": (128, 57088, 58624), "aw_ae": (64, 58624, 58880)},
    "mo1a": dict(_LN_XH_BF, **{
        "mo_ew": (128, 26112, 42496), "mo_gw": (128, 42496, 42624),
        "mo_eb": (1, 42624, 44672), "mo_t": (128, 44672, 45696),
        "mo_abf": (128, 45696, 46720)}),
    "mb1": {
        "ab_ab": (128, 0, 4096), "ab_r2": (128, 4096, 6144),
        "ab_xv": (128, 6144, 10240),
        "mb_dl": (128, 6144, 8192), "mb_u": (128, 8192, 10240),
        "mb_bq": (128, 10240, 12288), "mb_cq": (128, 12288, 14336),
        "mb_y": (128, 14336, 22528),
        "mb_zs": (128, 24576, 32768), "mb_xc": (128, 32768, 40960),
        "lnxb": (128, 24576, 28672), "lnsq": (128, 28672, 32768),
        "mb_ws": (128, 40960, 45056), "mb_br": (16, 45056, 46080),
        "mb_wb": (128, 46080, 46336), "mb_wc": (128, 46336, 46592),
        "mb_xtl": (128, 46592, 46688), "mb_xtl2": (128, 46688, 46784),
        "c_sel16": (16, 46784, 48832), "ab_xt": (128, 48832, 49344),
        "mb_b": (128, 49344, 51392), "mb_h": (128, 51392, 54464),
        "mb_tc": (128, 54464, 57536)},
}
_CARVE_F3 = {
    "aa0": dict(_LN_F3, **{"aa_nrf": (1, 1536, 1792),
                           "aa_bq": (128, 1792, 1800),
                           "aa_bk": (128, 1800, 1808)}),
    "f0a": dict(_LN_F3),
    "ab0": dict(_LN_F3, **{
        "ab_nrf": (1, 1536, 2560),
        "ab_bq_sb": (128, 2560, 2568), "ab_bk_sb": (128, 2568, 2576),
        "ab_bo_sb": (128, 2576, 2584), "ab_qn_sb": (128, 2584, 2592),
        "ab_kn_sb": (128, 2592, 2600)}),
    "aw1": dict(_LN_F3, **{"aw_nrf": (1, 1536, 1792)}),
    "mo1a": dict(_LN_F3, **{
        "mo_acc": (128, 1536, 5632), "mo_coef": (128, 5632, 5888),
        "mo_w": (128, 5888, 5920), "mo_z": (128, 5920, 5928),
        "mo_sc": (128, 5928, 5992)}),
    "mb1": dict(_LN_F3, **{
        "mb_cv": (128, 1536, 2048), "mb_a": (128, 1536, 3584),
        "mb_sp": (128, 3584, 4096),
        "mb_alog": (128, 4096, 4352), "mb_nega": (128, 4352, 4608),
        "mb_hc": (128, 4608, 4864),
        "mb_gam": (128, 4864, 4872), "mb_bet": (128, 4872, 4880),
        "mb_wcv": (128, 4880, 4944), "mb_bcv": (128, 4944, 4960),
        "mb_bdel": (128, 4960, 4976), "mb_dpar": (128, 4976, 4992)}),
}
for _al, _src in (("f0b", "f0a"), ("mo1b", "mo1a"), ("dbgln", "f0a")):
    _CARVE_BF[_al] = _CARVE_BF[_src]
    _CARVE_F3[_al] = _CARVE_F3[_src]


def _emit_step_loads(C, stp, w0=()):
    sb, hbm = C.sb, C.hbm
    t = []
    _load = C.load
    C.load = lambda *a, **k: _load(*a, waits=list(k.pop("waits", ())) + list(w0), **k)
    if stp == "aa0":
        t.append(C.load(sb["aa_bq"][:, :], hbm["aa_bq"][:, :]))
        t.append(C.load(sb["aa_bk"][:, :], hbm["aa_bk"][:, :]))
        t.append(C.load(sb["aa_bv"][:, :], hbm["aa_bv"][:, :], cast=True))
    elif stp == "ab0":
        for nm in ["ab_bq_sb", "ab_bk_sb", "ab_bo_sb", "ab_qn_sb", "ab_kn_sb"]:
            t.append(C.load(sb[nm][:, :], hbm[nm[:-3]][:, :]))
        t.append(C.load(sb["ab_bv"][:, :], hbm["ab_bv"][:, :], cast=True))
        t.append(C.load(sb["ab_ro"][:, 0:T], hbm["c_cos"][:, :], cast=True))
        t.append(C.load(sb["ab_ro"][:, T:2 * T], hbm["c_sin"][:, :], cast=True))
    elif stp == "aw1":
        pass
    elif stp in ("mo1a", "mo1b"):
        for c in range(NCH):
            t.append(C.load(sb["mo_gw"][:, c * 16:(c + 1) * 16],
                            hbm["mo_gw"][c * 128:(c + 1) * 128, :], cast=True))
    elif stp == "mb1":
        for nm in ["mb_gam", "mb_bet", "mb_wcv", "mb_bcv", "mb_alog",
                   "mb_bdel", "mb_dpar"]:
            t.append(C.load(sb[nm][:, :], hbm[nm][:, :]))
        t.append(C.load(sb["mb_wb"][:, :], hbm["mb_wb"][:, :], cast=True))
        t.append(C.load(sb["mb_wc"][:, :], hbm["mb_wc"][:, :], cast=True))
        t.append(C.load(sb["c_sel16"][:, :], hbm["c_sel16"][:, :], cast=True))
    C.load = _load
    return t


def emit_barrier(C):
    p = C.p
    toks = [p.tok("tensor"), p.tok("vector"), p.tok("scalar"), p.tok("gpsimd")]
    for e in ENGINES:
        p.emit(e, lambda h: h.nop(), waits=toks)


def build_program(steps, Ht):
    nc = bass.Bass("TRN2", target_bir_lowering=False, debug=False,
                   detect_race_conditions=False)
    names = ["x_t"]
    hbm = {
        "x_t": nc.dram_tensor("x_t", [D, T], F32, kind="ExternalInput").ap(),
        "y_t": nc.dram_tensor("y_t", [D, T], F32, kind="ExternalOutput").ap(),
    }
    decl_steps = [("f0a" if s.startswith("dbg") else s) for s in steps]
    for name, (arr, st) in Ht.items():
        if any(s in decl_steps for s in st):
            dt = I32 if arr.dtype == np.int32 else F32
            hbm[name] = nc.dram_tensor(name, list(arr.shape), dt,
                                       kind="ExternalInput").ap()
            names.append(name)

    prog = Prog()
    for e in ENGINES:
        prog.sem("s_" + e)
    prog.sem("q_st")

    abf_cols = max(max(e for (_, _, e) in _CARVE_BF[s].values()) for s in steps)
    af3_cols = max(max(e for (_, _, e) in _CARVE_F3[s].values()) for s in steps)

    with ExitStack() as es:
        sbuf = lambda n, shape, dt: es.enter_context(nc.sbuf_tensor("t_" + n, shape, dt))
        psum = lambda n, shape, dt: es.enter_context(nc.psum_tensor(n, shape, dt))
        sb = {}
        sb["XR"] = sbuf("XR", [128, NCH * T], F32)
        ABF = sbuf("ABF", [128, abf_cols], BF16)
        AF3 = sbuf("AF3", [128, af3_cols], F32)
        sb["cpool"] = sbuf("cpool", [128, 8], F32)
        sb["c_osum"] = sbuf("c_osum", [128, 128], BF16)
        sb["c_ones_col"] = sbuf("c_ones_col", [128, 1], BF16)
        sb["c_ones_row"] = sbuf("c_ones_row", [1, 128], BF16)
        sb["c_ones_row_f"] = sbuf("c_ones_row_f", [1, 128], F32)
        sb["c_ident"] = sbuf("c_ident", [128, 128], BF16)
        decl_has = lambda x: x in decl_steps
        if decl_has("ab0"):
            sb["c_rperm"] = sbuf("c_rperm", [128, 128], BF16)
        if decl_has("aw1"):
            sb["c_wmask"] = sbuf("c_wmask", [128, 384], BF16)

        ps = [psum(f"pb{i}", [128, 512], F32) for i in range(8)]

        C = Ctx(nc, prog, sb, ps, hbm)

        CPOOL_VALS = [1e-5, 1e-8, -30.0, 1.0, 0.0, 0.0, 0.0, 0.0]
        tokc = None
        for i, val in enumerate(CPOOL_VALS):
            tokc = prog.tick("vector", lambda v, i=i, val=val: v.memset(
                sb["cpool"][:, i:i + 1], float(val)))
        C.cpool_tok = tokc
        ctoks0 = [C.load(sb["c_osum"][:, :], hbm["c_osum"][:, :], cast=True,
                         key="cst")]
        ctoks0.append(C.load(sb["c_ones_col"][:, :], hbm["c_ones_col"][:, :],
                             cast=True, key="cst"))
        ctoks0.append(C.load(sb["c_ones_row"][:, :], hbm["c_ones_row"][:, :],
                             cast=True, key="cst"))
        ctoks0.append(C.load(sb["c_ones_row_f"][:, :], hbm["c_ones_row"][:, :],
                             key="cstp"))
        if "ab0" in decl_steps:
            ctoks0.append(C.load(sb["c_rperm"][:, :], hbm["c_rperm"][:, :],
                                 cast=True, key="cst"))
        if "aw1" in decl_steps:
            ctoks0.append(C.load(sb["c_wmask"][:, :], hbm["c_wmask"][:, :],
                                 cast=True, key="cst"))
        ctoks0.append(C.load(sb["c_ident"][:, :], hbm["c_ident"][:, :],
                             cast=True, key="cst"))
        ctoks = [(ctoks0[0][0], -1), ("q_cstp", -1)]
        xtoks = emit_load_x(C)

        cur = xtoks
        for si, stp in enumerate(steps):
            base = "f0a" if stp.startswith("dbg") else stp
            for nm, (p_, st_, en_) in _CARVE_BF[base].items():
                sb[nm] = ABF[0:p_, st_:en_]
            for nm, (p_, st_, en_) in _CARVE_F3[base].items():
                sb[nm] = AF3[0:p_, st_:en_]
            C.step_tag = str(si)
            prog.sem("q_cc" + C.step_tag)
            prog.sem("q_cp" + C.step_tag)
            prev = list(cur) if si > 0 else []
            wt = _emit_step_loads(C, base if stp.startswith("dbg") else stp,
                                  w0=prev)
            wtoks = ([("q_cc" + C.step_tag, -1), ("q_cp" + C.step_tag, -1)]
                     + ctoks + prev)
            C.wtoks = wtoks
            if stp == "aa0":
                lt = emit_ln(C, src_waits=cur, const_waits=ctoks)
                rt = emit_attn_a(C, lt, wtoks)
                cur = [rt[-1]]
            elif stp == "ab0":
                lt, r2t = emit_ln_coef(C, src_waits=cur, const_waits=ctoks,
                                       rms_eps=1e-8)
                rt = emit_attn_b(C, lt, r2t, wtoks)
                cur = [rt[-1]]
            elif stp == "aw1":
                lt = emit_ln(C, src_waits=cur, const_waits=ctoks)
                rt = emit_attn_w(C, lt, wtoks)
                cur = [rt[-1]]
            elif stp in ("mo1a", "mo1b"):
                lt = emit_ln(C, src_waits=cur, const_waits=ctoks)
                rt = emit_moe(C, lt, wtoks)
                cur = [rt[-1]]
            elif stp == "mb1":
                lt, r2t = emit_ln_coef(C, src_waits=cur, const_waits=ctoks,
                                       rms_eps=1e-5)
                rt = emit_mamba(C, lt, r2t, wtoks)
                cur = [rt[-1]]
            elif stp in ("f0a", "f0b"):
                lt = emit_ln(C, src_waits=cur, const_waits=ctoks)
                rt = emit_ffn(C, lt)
                cur = [rt[-1]]
            elif stp == "dbgln":
                lt = emit_ln(C, src_waits=cur, const_waits=ctoks)
                tdb = None
                for c in range(NCH):
                    tdb = prog.tick("vector", lambda v, c=c: v.tensor_copy(
                        out=XRc(C, c), in_=XHc(C, c)), waits=[lt[-1]])
                cur = [tdb]
            else:
                raise NotImplementedError(stp)

        emit_store_x(C, waits=cur)

        semh = {n: es.enter_context(nc.semaphore(n)) for n in prog.sem_names}
        blk = es.enter_context(nc.Block())

        def mk(eng):
            def body(h):
                waited = {}
                for fn, waits, incs in prog.ops[eng]:
                    for s, v in waits:
                        if v == -1:
                            v = prog.semc[s]
                        if waited.get(s, -2) < v:
                            h.wait_ge(semh[s], v)
                            waited[s] = v
                    instr = fn(h)
                    for s, a in incs:
                        instr = instr.then_inc(semh[s], a)
                if eng == "gpsimd":
                    h.wait_ge(semh["q_st"], prog.semc["q_st"])
            return body

        blk.tensor(mk("tensor"))
        blk.vector(mk("vector"))
        blk.scalar(mk("scalar"))
        blk.gpsimd(mk("gpsimd"))
        blk.sync(mk("sync"))

    return nc, names


def _run_steps(steps_groups, inputs, n_cores=8):
    x = np.asarray(inputs["L0_attn_a_x"], np.float32)
    Ht = _host_tensors(inputs)
    xs = [np.ascontiguousarray(x[i].T) for i in range(n_cores)]
    for grp in steps_groups:
        nc, names = build_program(grp, Ht)
        in_maps = []
        for i in range(n_cores):
            m = {"x_t": xs[i]}
            for n in names:
                if n != "x_t":
                    m[n] = Ht[n][0]
            in_maps.append(m)
        res = run_bass_kernel_spmd(nc, in_maps, core_ids=list(range(n_cores)))
        xs = [np.asarray(res.results[i]["y_t"]) for i in range(n_cores)]
    out = np.stack([xs[i].T for i in range(n_cores)])
    return np.ascontiguousarray(out.astype(np.float32))


def kernel(**inputs):
    steps_groups = [["aa0", "f0a", "ab0", "f0b", "aw1", "mo1a", "mb1", "mo1b"]]
    return _run_steps(steps_groups, inputs)
